# version 19
# baseline (speedup 1.0000x reference)
import numpy as np
import concourse.bass as bass
import concourse.mybir as mybir
import concourse.bass_utils as _BU
from concourse.bass_utils import run_bass_kernel_spmd
from contextlib import ExitStack
from pathlib import Path


def _bir_verify_and_optimise_ldw(tmpdir, inp="bir.json", outp="file.neff", arch=None, *,
                                 dve_root=None):
    cmd = [
        _BU.get_walrus_driver(),
        "--pass",
        ",".join(["birverifier", "runtime_memory_reservation", "lower_act", "lower_dve",
                  "lower_ap_offset", "codegen", "neff_packager"]),
        "-i", inp,
        "--neff-output-filename", outp,
        "--enable-birsim=true",
        "--mem-mode=physical",
        "--policy=0",
        "--enable-ldw-opt=true",
        "--assign-static-dmas-to-sp=false",
        "--dram-page-size=%s" % _BU.aot_getenv("NEURON_SCRATCHPAD_PAGE_SIZE", "256"),
        "--enable-neff-debug-info=true",
        "--jobs", "8",
        *_BU.get_walrus_args(_BU.get_bir_arch(tmpdir, inp) if arch is None else arch,
                             tmpdir, dve_root=dve_root),
    ]
    result = _BU.run_command(cmd, cwd=tmpdir)
    if result is not None:
        (Path(tmpdir) / "log.txt").write_text(result.stdout)
    return f"{tmpdir}/{outp}"


_BU.bir_verify_and_optimise = _bir_verify_and_optimise_ldw

F32 = mybir.dt.float32
BF16 = mybir.dt.bfloat16
FP8 = mybir.dt.float8e4
AF = mybir.ActivationFunctionType
ALU = mybir.AluOpType
AX = mybir.AxisListType

D = 1024
NCH = 8
TPS = 16
NSEQ = 2
NT = TPS * NSEQ
SEL = 4
H = 16
DH = 64
M = 512
EPS = 1e-5
NSC = 3

ENG_SEM = {"pe": "pe", "act": "act", "dve": "dve", "gp": "pool", "sync": None}


class Buf:
    __slots__ = ("w", "r")

    def __init__(self):
        self.w = None
        self.r = []

    def __repr__(self):
        return f"Buf(w={self.w})"


class Op:
    __slots__ = ("oid", "eng", "build", "dma", "cost", "deps", "succ", "prio",
                 "qpos", "start", "finish")

    def __init__(self, oid, eng, build, dma, cost, deps):
        self.oid = oid
        self.eng = eng
        self.build = build
        self.dma = dma
        self.cost = cost
        self.deps = deps
        self.succ = []
        self.prio = 0.0
        self.qpos = -1
        self.start = 0.0
        self.finish = 0.0


DMA_INFLIGHT = 8
NROT = 12


class Prog:

    def __init__(self, sems):
        self.sems = sems
        self.ops = []

    def emit(self, eng, build, reads=(), writes=(), dma=None, cost=None):
        if cost is None:
            cost = {"pe": 280.0, "act": 650.0, "dve": 700.0, "gp": 1900.0,
                    "sync": 1500.0}[eng]
            if dma is not None:
                cost = 1500.0
        oid = len(self.ops)
        deps = set()
        for b in reads:
            if b.w is not None:
                deps.add(b.w)
        for b in writes:
            if b.w is not None:
                deps.add(b.w)
            for t in b.r:
                deps.add(t)
        deps.discard(oid)
        op = Op(oid, eng, build, dma, float(cost), deps)
        self.ops.append(op)
        for b in reads:
            b.r.append(oid)
        for b in writes:
            b.w = oid
            b.r = []
        return oid

    def schedule(self):
        import heapq
        ops = self.ops
        n = len(ops)
        for op in ops:
            for d in op.deps:
                ops[d].succ.append(op.oid)
        for op in reversed(ops):
            m = 0.0
            for s in op.succ:
                if ops[s].prio > m:
                    m = ops[s].prio
            op.prio = m + op.cost
        ndep = [len(op.deps) for op in ops]
        eng_time = {e: 0.0 for e in ENG_SEM}
        ready = []
        for op in ops:
            if ndep[op.oid] == 0:
                heapq.heappush(ready, (0.0, -op.prio, op.oid))
        streams = {e: [] for e in ENG_SEM}
        order = []
        while ready:
            rt, nprio, oid = heapq.heappop(ready)
            op = ops[oid]
            start = max(rt, eng_time[op.eng])
            op.start = start
            op.finish = start + op.cost
            eng_time[op.eng] = op.finish
            streams[op.eng].append(oid)
            order.append(oid)
            for s in op.succ:
                ndep[s] -= 1
                if ndep[s] == 0:
                    srt = 0.0
                    se = ops[s].eng
                    for d in ops[s].deps:
                        f = ops[d].finish
                        if ops[d].eng != se or ops[d].dma is not None:
                            f += 500.0
                        if f > srt:
                            srt = f
                    heapq.heappush(ready, (srt, -ops[s].prio, s))
        assert len(order) == n, f"cycle in dep graph: {len(order)}/{n}"
        self.makespan = max(eng_time.values())
        return streams, order

    def build_plan(self):
        streams, order = self.schedule()
        ops = self.ops
        opos = {oid: i for i, oid in enumerate(order)}
        qhist = {"qs": [], "qg": []}
        semof = {}
        for oid in order:
            op = ops[oid]
            if op.dma is None:
                semof[oid] = ENG_SEM[op.eng]
            else:
                q = op.dma
                j = len(qhist[q])
                qhist[q].append(oid)
                op.qpos = j
                semof[oid] = "%s%d" % (q, j % NROT)
        needed = set()
        for oid in order:
            op = ops[oid]
            own = "pe" if (op.eng == "pe" and op.dma is None) else None
            best = {}
            for d in op.deps:
                s = semof[d]
                if s == own:
                    continue
                if s not in best or opos[d] > opos[best[s]]:
                    best[s] = d
            for d in best.values():
                needed.add(d)
            if op.dma is not None and op.qpos >= DMA_INFLIGHT:
                needed.add(qhist[op.dma][op.qpos - DMA_INFLIGHT])
        cnt = {k: 0 for k in self.sems}
        tag = {}
        plan = {e: [] for e in streams}
        marks = {e: {k: 0 for k in self.sems} for e in streams}
        for oid in order:
            op = ops[oid]
            e = op.eng
            s_name = semof[oid]
            do_inc = (op.dma is not None) or (oid in needed)
            inc = 16 if op.dma is not None else 1
            if do_inc:
                cnt[s_name] += inc
            tag[oid] = (s_name, cnt[s_name])
            own = "pe" if (e == "pe" and op.dma is None) else None
            need = {}
            for d in op.deps:
                s, v = tag[d]
                if s == own:
                    continue
                if need.get(s, 0) < v:
                    need[s] = v
            if op.dma is not None and op.qpos >= DMA_INFLIGHT:
                s, v = tag[qhist[op.dma][op.qpos - DMA_INFLIGHT]]
                if need.get(s, 0) < v:
                    need[s] = v
            waits = []
            for s, v in sorted(need.items()):
                if v > 0 and marks[e][s] < v:
                    waits.append((self.sems[s], v))
                    marks[e][s] = v
            sem_h = self.sems[s_name] if do_inc else None

            def action(E, waits=waits, build=op.build, sem_h=sem_h, inc=inc):
                for sh, v in waits:
                    E.wait_ge(sh, v)
                ins = build(E)
                if sem_h is not None:
                    ins.then_inc(sem_h, inc)

            plan[e].append(action)
        return plan, cnt


def build_program(debug=False):
    nc = bass.Bass("TRN2", target_bir_lowering=False)

    xin = nc.dram_tensor("xin", [NT * 128, D], F32, kind="ExternalInput").ap()
    w_in_h = nc.dram_tensor("w_in", [3 * D, D], F32, kind="ExternalInput").ap()
    b_in_h = nc.dram_tensor("b_in", [1, 3 * D], F32, kind="ExternalInput").ap()
    w_out_h = nc.dram_tensor("w_out", [D, D], F32, kind="ExternalInput").ap()
    b_out_h = nc.dram_tensor("b_out", [1, D], F32, kind="ExternalInput").ap()
    fw_h = nc.dram_tensor("fw", [D, D], F32, kind="ExternalInput").ap()
    fb_h = nc.dram_tensor("fb", [1, D], F32, kind="ExternalInput").ap()
    id_h = nc.dram_tensor("identm", [128, 128], F32, kind="ExternalInput").ap()
    cm_h = nc.dram_tensor("cmaskm", [128, 128], F32, kind="ExternalInput").ap()
    xout = nc.dram_tensor("xout", [NT * 128, D], F32, kind="ExternalOutput").ap()
    dbg = {}
    if debug:
        for nm, shape, dt in [("d_x1", [128, D], F32), ("d_xnt", [128, 8 * M], BF16),
                              ("d_q", [128, 8 * M], BF16), ("d_v", [128, SEL * D], BF16),
                              ("d_o", [128, D], BF16), ("d_x2", [128, D], F32)]:
            dbg[nm] = nc.dram_tensor(nm, shape, dt, kind="ExternalOutput").ap()

    with ExitStack() as ES:
        sb = lambda name, shape, dt: ES.enter_context(nc.sbuf_tensor(name, shape, dt))
        ps = lambda name, shape, dt: ES.enter_context(nc.psum_tensor(name, shape, dt))

        wiT = sb("wiT", [128, NCH * 3 * D], BF16)
        woT = sb("woT", [128, NCH * D], BF16)
        fwT = sb("fwT", [128, NCH * D], BF16)
        biasq = sb("biasq", [1, 3 * D], BF16)
        biaso = sb("biaso", [1, D], BF16)
        biasf = sb("biasf", [1, D], BF16)
        ones = sb("onesr", [1, 512], BF16)
        identf = sb("identf", [128, 128], F32)
        identb = sb("identb", [128, 128], BF16)
        cmask = sb("cmaskt", [128, 128], F32)
        epst = sb("epst", [128, 1], F32)
        biasfF = sb("biasfF", [128, D], BF16)
        bias_col = sb("bias_col", [128, 16], F32)

        xsel = sb("xsel", [128, 8 * D], F32)
        xns = sb("xns", [128, 4 * D], F32)

        xnb = sb("xnb", [128, D], F32)
        z8b = sb("z8b", [128, D], FP8)
        xdb = [sb("xdb%d" % i, [128, D], BF16) for i in range(NSC)]
        xdT = [sb("xdT%d" % i, [128, D], BF16) for i in range(NSC)]
        statb = [sb("statb%d" % i, [128, 128], F32) for i in range(NSC)]
        axn = [sb("axn0", [128, D], BF16)]
        oTb = sb("oTb", [128, D], BF16)

        xnT = sb("xnT", [128, NCH * M], BF16)
        qf = sb("qf", [128, NCH * M], BF16)
        kf = sb("kf", [128, NCH * M], BF16)
        vt = sb("vt", [128, SEL * H * 65], BF16)
        osb = sb("osb", [128, 2 * D], BF16)
        pb = [sb("pb%d" % i, [128, M], BF16) for i in range(2)]
        pTs = [sb("pTs%d" % i, [128, M], BF16) for i in range(2)]
        astat = sb("astat", [128, 64], F32)

        pA = ps("pA", [128, D], F32)
        pB = ps("pB", [128, D], F32)
        pt1 = ps("pt1", [128, D], BF16)
        pt2 = ps("pt2", [128, D], BF16)
        ps1 = ps("ps1", [128, 512], F32)
        ps2 = ps("ps2", [128, 512], F32)

        sems = {}
        for name in ("pe", "act", "dve", "pool"):
            sems[name] = ES.enter_context(nc.semaphore(name))
        for q in ("qs", "qg"):
            for j in range(NROT):
                sems["%s%d" % (q, j)] = ES.enter_context(nc.semaphore("%s%d" % (q, j)))

        P = Prog(sems)

        B_wiT = Buf(); B_woT = Buf(); B_fwT = Buf()
        B_biasq = Buf(); B_biaso = Buf(); B_biasf = Buf(); B_ones = Buf()
        B_identf = Buf(); B_identb = Buf(); B_cmask = Buf()
        B_biasfF = Buf(); B_bcol = Buf()
        B_eps = Buf()
        B_xsel = [Buf() for _ in range(8)]
        B_xns = [Buf() for _ in range(4)]
        B_xnb = Buf()
        B_z8 = Buf()
        B_xd = [Buf() for _ in range(NSC)]
        B_xdT = [Buf() for _ in range(NSC)]
        B_stat = [Buf() for _ in range(NSC)]
        B_axn = [Buf()]
        B_oT = Buf()
        B_xnT = Buf(); B_qf = Buf(); B_kf = Buf(); B_vt = Buf(); B_osb = [Buf(), Buf()]
        B_p = [Buf() for _ in range(2)]
        B_pT = [Buf() for _ in range(2)]
        B_astat = [Buf() for _ in range(4)]
        B_pA = [Buf(), Buf()]; B_scB = Buf(); B_pt1 = Buf(); B_pt2 = Buf()
        B_ps1 = Buf(); B_ps2 = Buf()

        wiT3 = wiT[:].rearrange("p (c e) -> p c e", c=NCH)
        woT3 = woT[:].rearrange("p (c e) -> p c e", c=NCH)
        fwT3 = fwT[:].rearrange("p (c e) -> p c e", c=NCH)
        xnT3 = xnT[:].rearrange("p (c t) -> p c t", c=NCH)
        qf3 = qf[:].rearrange("p (c t) -> p c t", c=NCH)
        kf3 = kf[:].rearrange("p (c t) -> p c t", c=NCH)
        vt3 = vt[:].rearrange("p (k h e) -> p k h e", k=SEL, h=H)
        osb3 = osb[:].rearrange("p (q c) -> p q c", q=2)
        xsel3 = xsel[:].rearrange("p (n c) -> p n c", n=8)
        xns3 = xns[:].rearrange("p (n c) -> p n c", n=4)

        wstage = [xns3[:, 0, :], xns3[:, 1, :]]
        B_wstage = [B_xns[0], B_xns[1]]

        uc = [0]

        P.emit("sync", lambda E: E.dma_start(out=identf[:], in_=id_h[:]), writes=[B_identf], dma="qs")
        P.emit("sync", lambda E: E.dma_start(out=cmask[:], in_=cm_h[:]), writes=[B_cmask], dma="qs")
        P.emit("act", lambda E: E.activation(identb[:], identf[:], AF.Copy),
               reads=[B_identf], writes=[B_identb])
        P.emit("dve", lambda E: E.memset(ones[:], 1.0), writes=[B_ones])
        P.emit("dve", lambda E: E.memset(epst[:], EPS), writes=[B_eps])

        def load_bias(src_ap, n, dstb, scales, Bdst):
            for c0 in range(0, n, D):
                sc = scales(c0)
                P.emit("sync", lambda E, c0=c0, src_ap=src_ap: E.dma_start(out=wstage[0][0:1, :], in_=src_ap[0:1, c0:c0 + D]),
                       writes=[B_wstage[0]], dma="qs")
                P.emit("act", lambda E, c0=c0, sc=sc, dstb=dstb: E.activation(dstb[0:1, c0:c0 + D], wstage[0][0:1, :],
                                                                   AF.Copy, scale=sc),
                       reads=[B_wstage[0]], writes=[Bdst])
        load_bias(b_in_h, 3 * D, biasq, lambda c0: 0.125 if c0 < D else 1.0, B_biasq)
        load_bias(b_out_h, D, biaso, lambda c0: 1.0, B_biaso)
        load_bias(fb_h, D, biasf, lambda c0: 0.5, B_biasf)

        for hf in range(2):
            P.emit("pe", lambda E, hf=hf: E.matmul(ps1[:], ones[0:1, 0:128],
                                                   biasf[0:1, hf * 512:(hf + 1) * 512],
                                                   start=True, stop=True),
                   reads=[B_ones, B_biasf], writes=[B_ps1], cost=280)
            P.emit("act", lambda E, hf=hf: E.activation(biasfF[:, hf * 512:(hf + 1) * 512], ps1[:], AF.Copy),
                   reads=[B_ps1], writes=[B_biasfF], cost=600)
        for j in range(16):
            P.emit("pe", lambda E, j=j: E.matmul(ps2[:, j:j + 1], biasq[0:1, j * 128:(j + 1) * 128],
                                                 ones[0:1, 0:1], start=(j == 0), stop=(j == 15),
                                                 skip_group_check=True),
                   reads=[B_biasq, B_ones], writes=[B_ps2], cost=100)
        P.emit("act", lambda E: E.activation(bias_col[:], ps2[:, 0:16], AF.Copy),
               reads=[B_ps2], writes=[B_bcol], cost=300)

        def quantize_mx(src_buf, src_ap, out_buf, out_ap, st, Bstat, dq_scale):
            am = st[:, 32:64]
            m3 = st[:, 64:96]
            inv = st[:, 96:128]
            src3 = src_ap.rearrange("p (b c) -> p b c", c=32)
            P.emit("dve", lambda E: E.tensor_reduce(am, src3, AX.X, ALU.max, apply_absolute_value=True),
                   reads=[src_buf], writes=[Bstat])
            P.emit("dve", lambda E: E.tensor_scalar(inv, am, float(64.0 / 7.5), float(64e-30), ALU.mult, ALU.max),
                   reads=[Bstat], writes=[Bstat])
            P.emit("dve", lambda E: E.reciprocal(inv, inv), reads=[Bstat], writes=[Bstat])
            P.emit("dve", lambda E: E.tensor_scalar(m3, am, float(64.0 * dq_scale / 7.5), float(64.0 * dq_scale * 1e-30), ALU.mult, ALU.max),
                   reads=[Bstat], writes=[Bstat])
            inv_b = inv.unsqueeze(2).broadcast_to([128, 32, 32])
            m3_b = m3.unsqueeze(2).broadcast_to([128, 32, 32])
            z3 = z8b[:].rearrange("p (b c) -> p b c", c=32)
            o3 = out_ap.rearrange("p (b c) -> p b c", c=32)
            P.emit("gp", lambda E: E.tensor_tensor(z3, src3, inv_b, ALU.mult),
                   reads=[src_buf, Bstat], writes=[B_z8], cost=1950)
            P.emit("gp", lambda E: E.tensor_tensor(o3, z3, m3_b, ALU.mult),
                   reads=[B_z8, Bstat], writes=[out_buf], cost=1950)

        def transpose_tile(src_buf, src_ap, dst_buf, dst_ap_3d, pt, Bpt):
            for ci in range(NCH):
                P.emit("pe", lambda E, ci=ci: E.transpose(pt[:, ci * 128:(ci + 1) * 128],
                                                          src_ap[:, ci * 128:(ci + 1) * 128], identb[:]),
                       reads=[src_buf, B_identb], writes=[Bpt], cost=240)
            P.emit("act", lambda E: E.activation(dst_ap_3d, pt[:].rearrange("p (c t) -> p c t", c=NCH), AF.Copy),
                   reads=[Bpt], writes=[dst_buf], cost=740)

        def weight_unit(i, src_h, row, dstb, dst3, mode, sc=1.0):
            wst, Bwst = wstage[i % 2], B_wstage[i % 2]
            par = i % NSC
            P.emit("sync", lambda E: E.dma_start(out=wst, in_=src_h[row * 128:(row + 1) * 128, :]),
                   writes=[Bwst], dma="qs")
            if mode == "quant":
                quantize_mx(Bwst, wst, B_xd[par], xdb[par][:], statb[par], B_stat[par], 0.5)
            else:
                P.emit("act", lambda E, sc=sc, par=par: E.activation(xdb[par][:], wst, AF.Copy, scale=sc),
                       reads=[Bwst], writes=[B_xd[par]])
            pt, Bpt = (pt1, B_pt1) if i % 2 == 0 else (pt2, B_pt2)
            transpose_tile(B_xd[par], xdb[par][:], dstb, dst3, pt, Bpt)

        wu = 0
        for oi in range(NCH):
            weight_unit(wu, fw_h, oi, B_fwT, fwT3[:, :, oi * 128:(oi + 1) * 128], "quant"); wu += 1
        for oi in range(NCH):
            weight_unit(wu, w_out_h, oi, B_woT, woT3[:, :, oi * 128:(oi + 1) * 128], "cast"); wu += 1
        for ei in range(3 * NCH):
            weight_unit(wu, w_in_h, ei, B_wiT, wiT3[:, :, ei * 128:(ei + 1) * 128], "cast",
                        0.125 if ei < NCH else 1.0); wu += 1

        def ffn(tile_buf, tile_ap):
            u = uc[0]; uc[0] += 1
            par = u % NSC
            st = statb[par]; Bst = B_stat[par]
            bns = st[:, 0:12]; mv = st[:, 12:14]; lnv = st[:, 16:17]; rstd = st[:, 17:18]
            P.emit("dve", lambda E: E.bn_stats(bns[:, 0:6], tile_ap[:, 0:512]),
                   reads=[tile_buf], writes=[Bst])
            P.emit("dve", lambda E: E.bn_stats(bns[:, 6:12], tile_ap[:, 512:1024]),
                   reads=[tile_buf], writes=[Bst])
            P.emit("dve", lambda E: E.bn_aggr(mv, bns), reads=[Bst], writes=[Bst])
            P.emit("act", lambda E: E.activation(lnv, mv[:, 1:2], AF.Ln, bias=epst[:, 0:1]),
                   reads=[Bst, B_eps], writes=[Bst])
            P.emit("act", lambda E: E.activation(rstd, lnv, AF.Exp, scale=-0.5),
                   reads=[Bst], writes=[Bst])
            P.emit("dve", lambda E: E.tensor_scalar(xnb[:], tile_ap, mv[:, 0:1], rstd, ALU.subtract, ALU.mult),
                   reads=[tile_buf, Bst], writes=[B_xnb])
            quantize_mx(B_xnb, xnb[:], B_xd[par], xdb[par][:], st, Bst, 1.0)
            transpose_tile(B_xd[par], xdb[par][:], B_xdT[par],
                           xdT[par][:].rearrange("p (c t) -> p c t", c=NCH), pt2, B_pt2)
            xdT3 = xdT[par][:].rearrange("p (c t) -> p c t", c=NCH)
            pX, BpX = pA, B_pA
            sls = [slice(0, 512), slice(512, 1024)]
            for ci in range(NCH):
                for hf in range(2):
                    P.emit("pe", lambda E, ci=ci, sl=sls[hf], pX=pX: E.matmul(
                        pX[:, sl], xdT3[:, ci, :], fwT3[:, ci, sl], start=(ci == 0), stop=(ci == NCH - 1),
                        skip_group_check=True),
                           reads=[B_xdT[par], B_fwT], writes=[BpX[hf]], cost=330)
            P.emit("gp", lambda E: E.tensor_tensor(tile_ap, tile_ap, biasfF[:], ALU.add),
                   reads=[B_biasfF, tile_buf], writes=[tile_buf], cost=1950)
            for hf in range(2):
                P.emit("dve", lambda E, sl=sls[hf], pX=pX: E.tensor_tensor(tile_ap[:, sl], tile_ap[:, sl],
                                                                           pX[:, sl], ALU.add),
                       reads=[BpX[hf], tile_buf], writes=[tile_buf], cost=660)

        def attn_ln(seq, qt):
            par = 0
            k = seq * SEL + qt
            tb = B_xsel[k]
            ta = xsel3[:, k, :]
            u = uc[0]; uc[0] += 1
            spar = u % NSC
            st = statb[spar]; Bst = B_stat[spar]
            bns = st[:, 0:12]; mv = st[:, 12:14]; lnv = st[:, 16:17]; rstd = st[:, 17:18]
            P.emit("dve", lambda E: E.bn_stats(bns[:, 0:6], ta[:, 0:512]), reads=[tb], writes=[Bst])
            P.emit("dve", lambda E: E.bn_stats(bns[:, 6:12], ta[:, 512:1024]), reads=[tb], writes=[Bst])
            P.emit("dve", lambda E: E.bn_aggr(mv, bns), reads=[Bst], writes=[Bst])
            P.emit("act", lambda E: E.activation(lnv, mv[:, 1:2], AF.Ln, bias=epst[:, 0:1]),
                   reads=[Bst, B_eps], writes=[Bst])
            P.emit("act", lambda E: E.activation(rstd, lnv, AF.Exp, scale=-0.5),
                   reads=[Bst], writes=[Bst])
            P.emit("dve", lambda E: E.tensor_scalar(axn[par][:], ta, mv[:, 0:1], rstd, ALU.subtract, ALU.mult),
                   reads=[tb, Bst], writes=[B_axn[par]])
            for ci in range(NCH):
                P.emit("pe", lambda E, ci=ci: E.transpose(pt1[:, ci * 128:(ci + 1) * 128],
                                                          axn[par][:, ci * 128:(ci + 1) * 128], identb[:]),
                       reads=[B_axn[par], B_identb], writes=[B_pt1])
            P.emit("act", lambda E, qt=qt: E.activation(xnT3[:, :, qt * 128:(qt + 1) * 128],
                                                        pt1[:].rearrange("p (c t) -> p c t", c=NCH), AF.Copy),
                   reads=[B_pt1], writes=[B_xnT])

        def qkv_qk(ei):
            pS, BpS = (ps1, B_ps1) if ei % 2 == 0 else (ps2, B_ps2)
            dst_ap = qf3[:, ei, :] if ei < NCH else kf3[:, ei - NCH, :]
            dstb = B_qf if ei < NCH else B_kf
            for ci in range(NCH):
                P.emit("pe", lambda E, ci=ci: E.matmul(pS[:], wiT3[:, ci, ei * 128:(ei + 1) * 128],
                                                       xnT3[:, ci, :], start=(ci == 0), stop=(ci == NCH - 1)),
                       reads=[B_wiT, B_xnT], writes=[BpS])
            P.emit("act", lambda E: E.activation(dst_ap, pS[:], AF.Identity, bias=bias_col[:, ei:ei + 1]),
                   reads=[BpS, B_bcol], writes=[dstb])

        def qkv_v(kt, hf):
            pS, BpS = (ps1, B_ps1) if hf == 0 else (ps2, B_ps2)
            esl = slice(2 * D + hf * 512, 2 * D + (hf + 1) * 512)
            osl = slice(hf * 512, (hf + 1) * 512)
            P.emit("pe", lambda E: E.matmul(pS[:], ones[0:1, 0:128], biasq[0:1, esl],
                                            start=True, stop=False),
                   reads=[B_ones, B_biasq], writes=[BpS])
            for ci in range(NCH):
                P.emit("pe", lambda E, ci=ci: E.matmul(pS[:], xnT3[:, ci, kt * 128:(kt + 1) * 128],
                                                       wiT3[:, ci, esl], start=False, stop=(ci == NCH - 1)),
                       reads=[B_xnT, B_wiT], writes=[BpS])
            P.emit("act", lambda E: E.activation(vt3[:, kt, hf * 8:(hf + 1) * 8, 0:64],
                                                 pS[:].rearrange("p (h e) -> p h e", h=8), AF.Copy),
                   reads=[BpS], writes=[B_vt])

        def pair_scores(seq, h0, qt):
            hc = h0 // 2
            W = (qt + 1) * 128

            def build2(E):
                E.matmul(ps1[:, 0:W], qf3[:, hc, qt * 128:(qt + 1) * 128][0:64, :],
                         kf3[:, hc, 0:W][0:64, :], start=True, stop=True,
                         skip_group_check=True)
                return E.matmul(pB[:, 0:W], qf3[:, hc, qt * 128:(qt + 1) * 128][64:128, :],
                                kf3[:, hc, 0:W][64:128, :], start=True, stop=True,
                                skip_group_check=True)
            P.emit("pe", build2, reads=[B_qf, B_kf], writes=[B_ps1, B_scB],
                   cost=150 + W * 0.42)

        def attn_head_qt(seq, h, qt):
            hc = h // 2
            hp = (h % 2) * 64
            W = (qt + 1) * 128
            par = h % 2
            pbuf, Bp = pb[par], B_p[par]
            pT, BpT = pTs[par], B_pT[par]
            if h % 2 == 0:
                psc, Bsc = ps1, B_ps1
            else:
                psc, Bsc = pB[:, 0:512], B_scB
            ppv, Bpv, pvo = ps2, B_ps2, 0
            aslot = h % 4
            Bast = B_astat[aslot]
            a0 = aslot * 16
            negmx = astat[:, a0 + 1:a0 + 2]
            sm = astat[:, a0 + 2:a0 + 3]; invs = astat[:, a0 + 3:a0 + 4]
            P.emit("pe", lambda E: E.matmul(psc[:, 0:W], qf3[:, hc, qt * 128:(qt + 1) * 128][hp:hp + 64, :],
                                            kf3[:, hc, 0:W][hp:hp + 64, :], start=True, stop=True,
                                            skip_group_check=True),
                   reads=[B_qf, B_kf], writes=[Bsc])
            P.emit("dve", lambda E: E.tensor_tensor(psc[:, W - 128:W], psc[:, W - 128:W], cmask[:], ALU.add),
                   reads=[B_cmask], writes=[Bsc])
            P.emit("dve", lambda E: E.tensor_reduce(negmx, psc[:, 0:W], AX.X, ALU.max, negate=True),
                   reads=[Bsc], writes=[Bast])
            P.emit("act", lambda E: E.activation(pbuf[:, 0:W], psc[:, 0:W], AF.Exp, bias=negmx),
                   reads=[Bsc, Bast], writes=[Bp])

            for kt in range(qt + 1):
                P.emit("pe", lambda E, kt=kt: E.transpose(pt1[:, kt * 128:(kt + 1) * 128],
                                                          pbuf[:, kt * 128:(kt + 1) * 128], identb[:]),
                       reads=[Bp, B_identb], writes=[B_pt1])
            P.emit("act", lambda E: E.activation(pT[:, 0:W], pt1[:, 0:W], AF.Copy),
                   reads=[B_pt1], writes=[BpT])
            for kt in range(qt + 1):
                P.emit("pe", lambda E, kt=kt: E.matmul(ppv[:, pvo:pvo + 65], pT[:, kt * 128:(kt + 1) * 128],
                                                       vt3[:, kt, h, :],
                                                       start=(kt == 0), stop=(kt == qt),
                                                       skip_group_check=True),
                       reads=[BpT, B_vt], writes=[Bpv])
            P.emit("dve", lambda E: E.reciprocal(invs, ppv[:, pvo + 64:pvo + 65]),
                   reads=[Bpv], writes=[Bast], cost=160)
            P.emit("dve", lambda E: E.tensor_scalar(osb3[:, qt % 2, h * 64:(h + 1) * 64], ppv[:, pvo:pvo + 64],
                                                    invs, None, ALU.mult),
                   reads=[Bpv, Bast], writes=[B_osb[qt % 2]])

        def outproj(seq, qt):
            u = uc[0]; uc[0] += 1
            k = seq * SEL + qt
            tb = B_xsel[k]
            ta = xsel3[:, k, :]
            for ci in range(NCH):
                P.emit("pe", lambda E, ci=ci: E.transpose(pt1[:, ci * 128:(ci + 1) * 128],
                                                          osb3[:, qt % 2, ci * 128:(ci + 1) * 128], identb[:]),
                       reads=[B_osb[qt % 2], B_identb], writes=[B_pt1])
            P.emit("act", lambda E: E.activation(oTb[:], pt1[:], AF.Copy),
                   reads=[B_pt1], writes=[B_oT])
            oT3 = oTb[:].rearrange("p (c t) -> p c t", c=NCH)
            pX, BpX = pA, B_pA
            sls = [slice(0, 512), slice(512, 1024)]
            for hf in range(2):
                P.emit("pe", lambda E, sl=sls[hf], pX=pX: E.matmul(pX[:, sl], ones[0:1, 0:128], biaso[0:1, sl],
                                                                   start=True, stop=False, skip_group_check=True),
                       reads=[B_ones, B_biaso], writes=[BpX[hf]], cost=280)
            for ci in range(NCH):
                for hf in range(2):
                    P.emit("pe", lambda E, ci=ci, sl=sls[hf], pX=pX: E.matmul(pX[:, sl], oT3[:, ci, :],
                                                                              woT3[:, ci, sl],
                                                                              start=False, stop=(ci == NCH - 1),
                                                                              skip_group_check=True),
                           reads=[B_oT, B_woT], writes=[BpX[hf]], cost=330)
            for hf in range(2):
                P.emit("dve", lambda E, sl=sls[hf], pX=pX: E.tensor_tensor(ta[:, sl], ta[:, sl], pX[:, sl], ALU.add),
                       reads=[BpX[hf], tb], writes=[tb], cost=660)

        def dump(name, buf, ap):
            if debug and name in dbg:
                P.emit("gp", lambda E: E.dma_start(out=dbg[name][:], in_=ap), reads=[buf], dma="qg")

        def load_tile(g, buf, ap):
            P.emit("sync", lambda E: E.dma_start(out=ap, in_=xin[g * 128:(g + 1) * 128, :]),
                   writes=[buf], dma="qs")

        def store_tile(g, buf, ap):
            P.emit("gp", lambda E: E.dma_start(out=xout[g * 128:(g + 1) * 128, :], in_=ap),
                   reads=[buf], dma="qg")

        for seq in range(NSEQ):
            for qt in range(SEL):
                load_tile(seq * TPS + qt, B_xsel[seq * SEL + qt], xsel3[:, seq * SEL + qt, :])

            A = []
            Fx = []

            for qt in range(SEL):
                A.append(lambda qt=qt: ffn(B_xsel[seq * SEL + qt], xsel3[:, seq * SEL + qt, :]))
            if debug and seq == 0:
                A.append(lambda: dump("d_x1", B_xsel[0], xsel3[:, 0, :]))
            for qt in range(SEL):
                A.append(lambda qt=qt: attn_ln(seq, qt))
            if debug and seq == 0:
                A.append(lambda: dump("d_xnt", B_xnT, xnT[:]))
            for ei in range(2 * NCH):
                A.append(lambda ei=ei: qkv_qk(ei))
            A.append(lambda: P.emit("dve", lambda E: E.memset(vt[:], 1.0), writes=[B_vt], cost=1150))
            for kt in range(SEL):
                for hf in range(2):
                    A.append(lambda kt=kt, hf=hf: qkv_v(kt, hf))
            if debug and seq == 0:
                A.append(lambda: dump("d_q", B_qf, qf[:]))
                A.append(lambda: dump("d_v", B_vt, vt[:]))
            def mk_self2(qt):
                def self2():
                    k = seq * SEL + qt
                    ffn(B_xsel[k], xsel3[:, k, :])
                    store_tile(seq * TPS + qt, B_xsel[k], xsel3[:, k, :])
                return self2
            def head_u(seq_, h, qt):
                if h % 2 == 0:
                    pair_scores(seq_, h, qt)
                attn_head_qt(seq_, h, qt)
            for h in range(H):
                A.append(lambda h=h: head_u(seq, h, 0))
            for qt in range(1, SEL + 1):
                if qt < SEL:
                    for h in range(4):
                        A.append(lambda h=h, qt=qt: head_u(seq, h, qt))
                A.append(lambda qt=qt: outproj(seq, qt - 1))
                A.append(mk_self2(qt - 1))
                if qt < SEL:
                    for h in range(4, H):
                        A.append(lambda h=h, qt=qt: head_u(seq, h, qt))

            nstiles = list(range(SEL, TPS))
            for gi in range(0, len(nstiles), 4):
                grp = nstiles[gi:gi + 4]
                for t in grp:
                    def ldf1(t=t):
                        g = seq * TPS + t
                        slot = t % 4
                        load_tile(g, B_xns[slot], xns3[:, slot, :])
                        ffn(B_xns[slot], xns3[:, slot, :])
                    Fx.append(ldf1)
                for t in grp:
                    def f2st(t=t):
                        g = seq * TPS + t
                        slot = t % 4
                        ffn(B_xns[slot], xns3[:, slot, :])
                        store_tile(g, B_xns[slot], xns3[:, slot, :])
                    Fx.append(f2st)

            ai, fi = 0, 0
            while ai < len(A) or fi < len(Fx):
                for _ in range(4):
                    if ai < len(A):
                        A[ai](); ai += 1
                if fi < len(Fx):
                    Fx[fi](); fi += 1

        plan, cnt = P.build_plan()
        def final_wait(E):
            for j in range(NROT):
                nm = "qg%d" % j
                if cnt[nm]:
                    E.wait_ge(sems[nm], cnt[nm])
        plan["gp"].append(final_wait)

        with nc.Block() as block:
            @block.sync
            def _(E):
                for a in plan["sync"]:
                    a(E)

            @block.tensor
            def _(E):
                for a in plan["pe"]:
                    a(E)

            @block.vector
            def _(E):
                for a in plan["dve"]:
                    a(E)

            @block.scalar
            def _(E):
                for a in plan["act"]:
                    a(E)

            @block.gpsimd
            def _(E):
                for a in plan["gp"]:
                    a(E)

    return nc


_CACHE = {}


def kernel(L0_attn_x, L0_attn_w_in, L0_attn_b_in, L0_attn_w_out, L0_attn_b_out,
           L0_ffn_weight, L0_ffn_bias):
    x = np.ascontiguousarray(np.asarray(L0_attn_x), dtype=np.float32)
    w_in = np.ascontiguousarray(np.asarray(L0_attn_w_in), dtype=np.float32)
    b_in = np.ascontiguousarray(np.asarray(L0_attn_b_in), dtype=np.float32).reshape(1, 3 * D)
    w_out = np.ascontiguousarray(np.asarray(L0_attn_w_out), dtype=np.float32)
    b_out = np.ascontiguousarray(np.asarray(L0_attn_b_out), dtype=np.float32).reshape(1, D)
    fw = np.ascontiguousarray(np.asarray(L0_ffn_weight), dtype=np.float32)
    fb = np.ascontiguousarray(np.asarray(L0_ffn_bias), dtype=np.float32).reshape(1, D)

    if "nc" not in _CACHE:
        _CACHE["nc"] = build_program()
    nc = _CACHE["nc"]

    ident = np.eye(128, dtype=np.float32)
    qi = np.arange(128)
    cmask = np.where(qi[None, :] <= qi[:, None], 0.0, -1e9).astype(np.float32)

    in_maps = []
    for i in range(8):
        xs = x[2 * i:2 * i + 2]
        xp = np.ascontiguousarray(
            xs.reshape(2, 512, 4, D).transpose(0, 2, 1, 3).reshape(NT * 128, D))
        in_maps.append({
            "xin": xp, "w_in": w_in, "b_in": b_in, "w_out": w_out, "b_out": b_out,
            "fw": fw, "fb": fb, "identm": ident, "cmaskm": cmask,
        })

    res = run_bass_kernel_spmd(nc, in_maps, core_ids=list(range(8)))
    out = np.empty((16, 2048, D), dtype=np.float32)
    for i in range(8):
        xo = res.results[i]["xout"]
        out[2 * i:2 * i + 2] = xo.reshape(2, 4, 512, D).transpose(0, 2, 1, 3).reshape(2, 2048, D)
    return out


# version 20
# speedup vs baseline: 1.0283x; 1.0283x over previous
import numpy as np
import concourse.bass as bass
import concourse.mybir as mybir
import concourse.bass_utils as _BU
from concourse.bass_utils import run_bass_kernel_spmd
from contextlib import ExitStack
from pathlib import Path


def _bir_verify_and_optimise_ldw(tmpdir, inp="bir.json", outp="file.neff", arch=None, *,
                                 dve_root=None):
    cmd = [
        _BU.get_walrus_driver(),
        "--pass",
        ",".join(["birverifier", "runtime_memory_reservation", "lower_act", "lower_dve",
                  "lower_ap_offset", "codegen", "neff_packager"]),
        "-i", inp,
        "--neff-output-filename", outp,
        "--enable-birsim=true",
        "--mem-mode=physical",
        "--policy=0",
        "--enable-ldw-opt=true",
        "--assign-static-dmas-to-sp=false",
        "--dram-page-size=%s" % _BU.aot_getenv("NEURON_SCRATCHPAD_PAGE_SIZE", "256"),
        "--enable-neff-debug-info=true",
        "--jobs", "8",
        *_BU.get_walrus_args(_BU.get_bir_arch(tmpdir, inp) if arch is None else arch,
                             tmpdir, dve_root=dve_root),
    ]
    result = _BU.run_command(cmd, cwd=tmpdir)
    if result is not None:
        (Path(tmpdir) / "log.txt").write_text(result.stdout)
    return f"{tmpdir}/{outp}"


_BU.bir_verify_and_optimise = _bir_verify_and_optimise_ldw

F32 = mybir.dt.float32
BF16 = mybir.dt.bfloat16
FP8 = mybir.dt.float8e4
AF = mybir.ActivationFunctionType
ALU = mybir.AluOpType
AX = mybir.AxisListType

D = 1024
NCH = 8
TPS = 16
NSEQ = 2
NT = TPS * NSEQ
SEL = 4
H = 16
DH = 64
M = 512
EPS = 1e-5
NSC = 3

ENG_SEM = {"pe": "pe", "act": "act", "dve": "dve", "gp": "pool", "sync": None}


class Buf:
    __slots__ = ("w", "r")

    def __init__(self):
        self.w = None
        self.r = []

    def __repr__(self):
        return f"Buf(w={self.w})"


class Op:
    __slots__ = ("oid", "eng", "build", "dma", "cost", "deps", "succ", "prio",
                 "qpos", "start", "finish")

    def __init__(self, oid, eng, build, dma, cost, deps):
        self.oid = oid
        self.eng = eng
        self.build = build
        self.dma = dma
        self.cost = cost
        self.deps = deps
        self.succ = []
        self.prio = 0.0
        self.qpos = -1
        self.start = 0.0
        self.finish = 0.0


DMA_INFLIGHT = 8
NROT = 12


class Prog:

    def __init__(self, sems):
        self.sems = sems
        self.ops = []

    def emit(self, eng, build, reads=(), writes=(), dma=None, cost=None):
        if cost is None:
            cost = {"pe": 280.0, "act": 650.0, "dve": 700.0, "gp": 1900.0,
                    "sync": 1500.0}[eng]
            if dma is not None:
                cost = 1500.0
        oid = len(self.ops)
        deps = set()
        for b in reads:
            if b.w is not None:
                deps.add(b.w)
        for b in writes:
            if b.w is not None:
                deps.add(b.w)
            for t in b.r:
                deps.add(t)
        deps.discard(oid)
        op = Op(oid, eng, build, dma, float(cost), deps)
        self.ops.append(op)
        for b in reads:
            b.r.append(oid)
        for b in writes:
            b.w = oid
            b.r = []
        return oid

    def schedule(self):
        import heapq
        ops = self.ops
        n = len(ops)
        for op in ops:
            for d in op.deps:
                ops[d].succ.append(op.oid)
        for op in reversed(ops):
            m = 0.0
            for s in op.succ:
                if ops[s].prio > m:
                    m = ops[s].prio
            op.prio = m + op.cost
        ndep = [len(op.deps) for op in ops]
        eng_time = {e: 0.0 for e in ENG_SEM}
        ready = []
        for op in ops:
            if ndep[op.oid] == 0:
                heapq.heappush(ready, (0.0, -op.prio, op.oid))
        streams = {e: [] for e in ENG_SEM}
        order = []
        while ready:
            rt, nprio, oid = heapq.heappop(ready)
            op = ops[oid]
            start = max(rt, eng_time[op.eng])
            op.start = start
            op.finish = start + op.cost
            eng_time[op.eng] = op.finish
            streams[op.eng].append(oid)
            order.append(oid)
            for s in op.succ:
                ndep[s] -= 1
                if ndep[s] == 0:
                    srt = 0.0
                    se = ops[s].eng
                    for d in ops[s].deps:
                        f = ops[d].finish
                        if ops[d].eng != se or ops[d].dma is not None:
                            f += 500.0
                        if f > srt:
                            srt = f
                    heapq.heappush(ready, (srt, -ops[s].prio, s))
        assert len(order) == n, f"cycle in dep graph: {len(order)}/{n}"
        self.makespan = max(eng_time.values())
        return streams, order

    def build_plan(self):
        streams, order = self.schedule()
        ops = self.ops
        opos = {oid: i for i, oid in enumerate(order)}
        qhist = {"qs": [], "qg": [], "qa": []}
        semof = {}
        for oid in order:
            op = ops[oid]
            if op.dma is None:
                semof[oid] = ENG_SEM[op.eng]
            else:
                q = op.dma
                j = len(qhist[q])
                qhist[q].append(oid)
                op.qpos = j
                semof[oid] = "%s%d" % (q, j % NROT)
        needed = set()
        for oid in order:
            op = ops[oid]
            own = "pe" if (op.eng == "pe" and op.dma is None) else None
            best = {}
            for d in op.deps:
                s = semof[d]
                if s == own:
                    continue
                if s not in best or opos[d] > opos[best[s]]:
                    best[s] = d
            for d in best.values():
                needed.add(d)
            if op.dma is not None and op.qpos >= DMA_INFLIGHT:
                needed.add(qhist[op.dma][op.qpos - DMA_INFLIGHT])
        cnt = {k: 0 for k in self.sems}
        tag = {}
        plan = {e: [] for e in streams}
        marks = {e: {k: 0 for k in self.sems} for e in streams}
        for oid in order:
            op = ops[oid]
            e = op.eng
            s_name = semof[oid]
            do_inc = (op.dma is not None) or (oid in needed)
            inc = 16 if op.dma is not None else 1
            if do_inc:
                cnt[s_name] += inc
            tag[oid] = (s_name, cnt[s_name])
            own = "pe" if (e == "pe" and op.dma is None) else None
            need = {}
            for d in op.deps:
                s, v = tag[d]
                if s == own:
                    continue
                if need.get(s, 0) < v:
                    need[s] = v
            if op.dma is not None and op.qpos >= DMA_INFLIGHT:
                s, v = tag[qhist[op.dma][op.qpos - DMA_INFLIGHT]]
                if need.get(s, 0) < v:
                    need[s] = v
            waits = []
            for s, v in sorted(need.items()):
                if v > 0 and marks[e][s] < v:
                    waits.append((self.sems[s], v))
                    marks[e][s] = v
            sem_h = self.sems[s_name] if do_inc else None

            def action(E, waits=waits, build=op.build, sem_h=sem_h, inc=inc):
                for sh, v in waits:
                    E.wait_ge(sh, v)
                ins = build(E)
                if sem_h is not None:
                    ins.then_inc(sem_h, inc)

            plan[e].append(action)
        return plan, cnt


def build_program(debug=False):
    nc = bass.Bass("TRN2", target_bir_lowering=False)

    xin = nc.dram_tensor("xin", [NT * 128, D], F32, kind="ExternalInput").ap()
    w_in_h = nc.dram_tensor("w_in", [3 * D, D], F32, kind="ExternalInput").ap()
    b_in_h = nc.dram_tensor("b_in", [1, 3 * D], F32, kind="ExternalInput").ap()
    w_out_h = nc.dram_tensor("w_out", [D, D], F32, kind="ExternalInput").ap()
    b_out_h = nc.dram_tensor("b_out", [1, D], F32, kind="ExternalInput").ap()
    fw_h = nc.dram_tensor("fw", [D, D], F32, kind="ExternalInput").ap()
    fb_h = nc.dram_tensor("fb", [1, D], F32, kind="ExternalInput").ap()
    id_h = nc.dram_tensor("identm", [128, 128], F32, kind="ExternalInput").ap()
    cm_h = nc.dram_tensor("cmaskm", [128, 128], F32, kind="ExternalInput").ap()
    xout = nc.dram_tensor("xout", [NT * 128, D], F32, kind="ExternalOutput").ap()
    dbg = {}
    if debug:
        for nm, shape, dt in [("d_x1", [128, D], F32), ("d_xnt", [128, 8 * M], BF16),
                              ("d_q", [128, 8 * M], BF16), ("d_v", [128, SEL * D], BF16),
                              ("d_o", [128, D], BF16), ("d_x2", [128, D], F32)]:
            dbg[nm] = nc.dram_tensor(nm, shape, dt, kind="ExternalOutput").ap()

    with ExitStack() as ES:
        sb = lambda name, shape, dt: ES.enter_context(nc.sbuf_tensor(name, shape, dt))
        ps = lambda name, shape, dt: ES.enter_context(nc.psum_tensor(name, shape, dt))

        wiT = sb("wiT", [128, NCH * 3 * D], BF16)
        woT = sb("woT", [128, NCH * D], BF16)
        fwT = sb("fwT", [128, NCH * D], BF16)
        biasq = sb("biasq", [1, 3 * D], BF16)
        biaso = sb("biaso", [1, D], BF16)
        biasf = sb("biasf", [1, D], BF16)
        ones = sb("onesr", [1, 512], BF16)
        identf = sb("identf", [128, 128], F32)
        identb = sb("identb", [128, 128], BF16)
        cmask = sb("cmaskt", [128, 128], F32)
        epst = sb("epst", [128, 1], F32)
        biasfF = sb("biasfF", [128, D], BF16)
        bias_col = sb("bias_col", [128, 16], F32)

        xsel = sb("xsel", [128, 8 * D], F32)
        xns = sb("xns", [128, 4 * D], F32)

        xnb = sb("xnb", [128, D], F32)
        z8b = sb("z8b", [128, D], FP8)
        xdb = [sb("xdb%d" % i, [128, D], BF16) for i in range(NSC)]
        xdT = [sb("xdT%d" % i, [128, D], BF16) for i in range(NSC)]
        statb = [sb("statb%d" % i, [128, 128], F32) for i in range(NSC)]
        axn = [sb("axn0", [128, D], BF16)]
        oTb = sb("oTb", [128, D], BF16)

        xnT = sb("xnT", [128, NCH * M], BF16)
        qf = sb("qf", [128, NCH * M], BF16)
        kf = sb("kf", [128, NCH * M], BF16)
        vt = sb("vt", [128, SEL * H * 65], BF16)
        osb = sb("osb", [128, 2 * D], BF16)
        pb = [sb("pb%d" % i, [128, M], BF16) for i in range(2)]
        pTs = [sb("pTs%d" % i, [128, M], BF16) for i in range(2)]
        astat = sb("astat", [128, 64], F32)

        pA = ps("pA", [128, D], F32)
        pB = ps("pB", [128, D], F32)
        pt1 = ps("pt1", [128, D], BF16)
        pt2 = ps("pt2", [128, D], BF16)
        ps1 = ps("ps1", [128, 512], F32)
        ps2 = ps("ps2", [128, 512], F32)

        sems = {}
        for name in ("pe", "act", "dve", "pool"):
            sems[name] = ES.enter_context(nc.semaphore(name))
        for q in ("qs", "qg", "qa"):
            for j in range(NROT):
                sems["%s%d" % (q, j)] = ES.enter_context(nc.semaphore("%s%d" % (q, j)))

        P = Prog(sems)

        B_wiT = Buf(); B_woT = Buf(); B_fwT = Buf()
        B_biasq = Buf(); B_biaso = Buf(); B_biasf = Buf(); B_ones = Buf()
        B_identf = Buf(); B_identb = Buf(); B_cmask = Buf()
        B_biasfF = Buf(); B_bcol = Buf()
        B_eps = Buf()
        B_xsel = [Buf() for _ in range(8)]
        B_xns = [Buf() for _ in range(4)]
        B_xnb = Buf()
        B_z8 = Buf()
        B_xd = [Buf() for _ in range(NSC)]
        B_xdT = [Buf() for _ in range(NSC)]
        B_stat = [Buf() for _ in range(NSC)]
        B_axn = [Buf()]
        B_oT = Buf()
        B_xnT = Buf(); B_qf = Buf(); B_kf = Buf(); B_vt = Buf(); B_osb = [Buf(), Buf()]
        B_p = [Buf() for _ in range(2)]
        B_pT = [Buf() for _ in range(2)]
        B_astat = [Buf() for _ in range(4)]
        B_pA = [Buf(), Buf()]; B_scB = Buf(); B_pt1 = Buf(); B_pt2 = Buf()
        B_ps1 = Buf(); B_ps2 = Buf()

        wiT3 = wiT[:].rearrange("p (c e) -> p c e", c=NCH)
        woT3 = woT[:].rearrange("p (c e) -> p c e", c=NCH)
        fwT3 = fwT[:].rearrange("p (c e) -> p c e", c=NCH)
        xnT3 = xnT[:].rearrange("p (c t) -> p c t", c=NCH)
        qf3 = qf[:].rearrange("p (c t) -> p c t", c=NCH)
        kf3 = kf[:].rearrange("p (c t) -> p c t", c=NCH)
        vt3 = vt[:].rearrange("p (k h e) -> p k h e", k=SEL, h=H)
        osb3 = osb[:].rearrange("p (q c) -> p q c", q=2)
        xsel3 = xsel[:].rearrange("p (n c) -> p n c", n=8)
        xns3 = xns[:].rearrange("p (n c) -> p n c", n=4)

        wstage = [xns3[:, 0, :], xns3[:, 1, :], xns3[:, 2, :]]
        B_wstage = [B_xns[0], B_xns[1], B_xns[2]]

        uc = [0]

        P.emit("sync", lambda E: E.dma_start(out=identf[:], in_=id_h[:]), writes=[B_identf], dma="qs")
        P.emit("sync", lambda E: E.dma_start(out=cmask[:], in_=cm_h[:]), writes=[B_cmask], dma="qs")
        P.emit("act", lambda E: E.activation(identb[:], identf[:], AF.Copy),
               reads=[B_identf], writes=[B_identb])
        P.emit("dve", lambda E: E.memset(ones[:], 1.0), writes=[B_ones])
        P.emit("dve", lambda E: E.memset(epst[:], EPS), writes=[B_eps])

        def load_bias(src_ap, n, dstb, scales, Bdst):
            for c0 in range(0, n, D):
                sc = scales(c0)
                P.emit("sync", lambda E, c0=c0, src_ap=src_ap: E.dma_start(out=wstage[0][0:1, :], in_=src_ap[0:1, c0:c0 + D]),
                       writes=[B_wstage[0]], dma="qs")
                P.emit("act", lambda E, c0=c0, sc=sc, dstb=dstb: E.activation(dstb[0:1, c0:c0 + D], wstage[0][0:1, :],
                                                                   AF.Copy, scale=sc),
                       reads=[B_wstage[0]], writes=[Bdst])
        load_bias(b_in_h, 3 * D, biasq, lambda c0: 0.125 if c0 < D else 1.0, B_biasq)
        load_bias(b_out_h, D, biaso, lambda c0: 1.0, B_biaso)
        load_bias(fb_h, D, biasf, lambda c0: 0.5, B_biasf)

        for hf in range(2):
            P.emit("pe", lambda E, hf=hf: E.matmul(ps1[:], ones[0:1, 0:128],
                                                   biasf[0:1, hf * 512:(hf + 1) * 512],
                                                   start=True, stop=True),
                   reads=[B_ones, B_biasf], writes=[B_ps1], cost=280)
            P.emit("act", lambda E, hf=hf: E.activation(biasfF[:, hf * 512:(hf + 1) * 512], ps1[:], AF.Copy),
                   reads=[B_ps1], writes=[B_biasfF], cost=600)
        for j in range(16):
            P.emit("pe", lambda E, j=j: E.matmul(ps2[:, j:j + 1], biasq[0:1, j * 128:(j + 1) * 128],
                                                 ones[0:1, 0:1], start=(j == 0), stop=(j == 15),
                                                 skip_group_check=True),
                   reads=[B_biasq, B_ones], writes=[B_ps2], cost=100)
        P.emit("act", lambda E: E.activation(bias_col[:], ps2[:, 0:16], AF.Copy),
               reads=[B_ps2], writes=[B_bcol], cost=300)

        def quantize_mx(src_buf, src_ap, out_buf, out_ap, st, Bstat, dq_scale):
            am = st[:, 32:64]
            m3 = st[:, 64:96]
            inv = st[:, 96:128]
            src3 = src_ap.rearrange("p (b c) -> p b c", c=32)
            P.emit("dve", lambda E: E.tensor_reduce(am, src3, AX.X, ALU.max, apply_absolute_value=True),
                   reads=[src_buf], writes=[Bstat])
            P.emit("dve", lambda E: E.tensor_scalar(inv, am, float(64.0 / 7.5), float(64e-30), ALU.mult, ALU.max),
                   reads=[Bstat], writes=[Bstat])
            P.emit("dve", lambda E: E.reciprocal(inv, inv), reads=[Bstat], writes=[Bstat])
            P.emit("dve", lambda E: E.tensor_scalar(m3, am, float(64.0 * dq_scale / 7.5), float(64.0 * dq_scale * 1e-30), ALU.mult, ALU.max),
                   reads=[Bstat], writes=[Bstat])
            inv_b = inv.unsqueeze(2).broadcast_to([128, 32, 32])
            m3_b = m3.unsqueeze(2).broadcast_to([128, 32, 32])
            z3 = z8b[:].rearrange("p (b c) -> p b c", c=32)
            o3 = out_ap.rearrange("p (b c) -> p b c", c=32)
            P.emit("gp", lambda E: E.tensor_tensor(z3, src3, inv_b, ALU.mult),
                   reads=[src_buf, Bstat], writes=[B_z8], cost=1950)
            P.emit("gp", lambda E: E.tensor_tensor(o3, z3, m3_b, ALU.mult),
                   reads=[B_z8, Bstat], writes=[out_buf], cost=1950)

        def transpose_tile(src_buf, src_ap, dst_buf, dst_ap_3d, pt, Bpt):
            for ci in range(NCH):
                P.emit("pe", lambda E, ci=ci: E.transpose(pt[:, ci * 128:(ci + 1) * 128],
                                                          src_ap[:, ci * 128:(ci + 1) * 128], identb[:]),
                       reads=[src_buf, B_identb], writes=[Bpt], cost=240)
            P.emit("act", lambda E: E.activation(dst_ap_3d, pt[:].rearrange("p (c t) -> p c t", c=NCH), AF.Copy),
                   reads=[Bpt], writes=[dst_buf], cost=740)

        def weight_unit(i, src_h, row, dstb, dst3, mode, sc=1.0):
            wst, Bwst = wstage[i % 3], B_wstage[i % 3]
            par = i % NSC
            P.emit("act", lambda E: E.dma_start(out=wst, in_=src_h[row * 128:(row + 1) * 128, :]),
                   writes=[Bwst], dma="qa")
            if mode == "quant":
                quantize_mx(Bwst, wst, B_xd[par], xdb[par][:], statb[par], B_stat[par], 0.5)
            else:
                P.emit("act", lambda E, sc=sc, par=par: E.activation(xdb[par][:], wst, AF.Copy, scale=sc),
                       reads=[Bwst], writes=[B_xd[par]])
            pt, Bpt = (pt1, B_pt1) if i % 2 == 0 else (pt2, B_pt2)
            transpose_tile(B_xd[par], xdb[par][:], dstb, dst3, pt, Bpt)

        wu = 0
        for oi in range(NCH):
            weight_unit(wu, fw_h, oi, B_fwT, fwT3[:, :, oi * 128:(oi + 1) * 128], "quant"); wu += 1
        for oi in range(NCH):
            weight_unit(wu, w_out_h, oi, B_woT, woT3[:, :, oi * 128:(oi + 1) * 128], "cast"); wu += 1
        for ei in range(3 * NCH):
            weight_unit(wu, w_in_h, ei, B_wiT, wiT3[:, :, ei * 128:(ei + 1) * 128], "cast",
                        0.125 if ei < NCH else 1.0); wu += 1

        def ffn(tile_buf, tile_ap):
            u = uc[0]; uc[0] += 1
            par = u % NSC
            st = statb[par]; Bst = B_stat[par]
            bns = st[:, 0:12]; mv = st[:, 12:14]; lnv = st[:, 16:17]; rstd = st[:, 17:18]
            P.emit("dve", lambda E: E.bn_stats(bns[:, 0:6], tile_ap[:, 0:512]),
                   reads=[tile_buf], writes=[Bst])
            P.emit("dve", lambda E: E.bn_stats(bns[:, 6:12], tile_ap[:, 512:1024]),
                   reads=[tile_buf], writes=[Bst])
            P.emit("dve", lambda E: E.bn_aggr(mv, bns), reads=[Bst], writes=[Bst])
            P.emit("act", lambda E: E.activation(lnv, mv[:, 1:2], AF.Ln, bias=epst[:, 0:1]),
                   reads=[Bst, B_eps], writes=[Bst])
            P.emit("act", lambda E: E.activation(rstd, lnv, AF.Exp, scale=-0.5),
                   reads=[Bst], writes=[Bst])
            P.emit("dve", lambda E: E.tensor_scalar(xnb[:], tile_ap, mv[:, 0:1], rstd, ALU.subtract, ALU.mult),
                   reads=[tile_buf, Bst], writes=[B_xnb])
            quantize_mx(B_xnb, xnb[:], B_xd[par], xdb[par][:], st, Bst, 1.0)
            transpose_tile(B_xd[par], xdb[par][:], B_xdT[par],
                           xdT[par][:].rearrange("p (c t) -> p c t", c=NCH), pt2, B_pt2)
            xdT3 = xdT[par][:].rearrange("p (c t) -> p c t", c=NCH)
            pX, BpX = pA, B_pA
            sls = [slice(0, 512), slice(512, 1024)]
            for ci in range(NCH):
                for hf in range(2):
                    P.emit("pe", lambda E, ci=ci, sl=sls[hf], pX=pX: E.matmul(
                        pX[:, sl], xdT3[:, ci, :], fwT3[:, ci, sl], start=(ci == 0), stop=(ci == NCH - 1),
                        skip_group_check=True),
                           reads=[B_xdT[par], B_fwT], writes=[BpX[hf]], cost=330)
            P.emit("gp", lambda E: E.tensor_tensor(tile_ap, tile_ap, biasfF[:], ALU.add),
                   reads=[B_biasfF, tile_buf], writes=[tile_buf], cost=1950)
            for hf in range(2):
                P.emit("dve", lambda E, sl=sls[hf], pX=pX: E.tensor_tensor(tile_ap[:, sl], tile_ap[:, sl],
                                                                           pX[:, sl], ALU.add),
                       reads=[BpX[hf], tile_buf], writes=[tile_buf], cost=660)

        def attn_ln(seq, qt):
            par = 0
            k = seq * SEL + qt
            tb = B_xsel[k]
            ta = xsel3[:, k, :]
            u = uc[0]; uc[0] += 1
            spar = u % NSC
            st = statb[spar]; Bst = B_stat[spar]
            bns = st[:, 0:12]; mv = st[:, 12:14]; lnv = st[:, 16:17]; rstd = st[:, 17:18]
            P.emit("dve", lambda E: E.bn_stats(bns[:, 0:6], ta[:, 0:512]), reads=[tb], writes=[Bst])
            P.emit("dve", lambda E: E.bn_stats(bns[:, 6:12], ta[:, 512:1024]), reads=[tb], writes=[Bst])
            P.emit("dve", lambda E: E.bn_aggr(mv, bns), reads=[Bst], writes=[Bst])
            P.emit("act", lambda E: E.activation(lnv, mv[:, 1:2], AF.Ln, bias=epst[:, 0:1]),
                   reads=[Bst, B_eps], writes=[Bst])
            P.emit("act", lambda E: E.activation(rstd, lnv, AF.Exp, scale=-0.5),
                   reads=[Bst], writes=[Bst])
            P.emit("dve", lambda E: E.tensor_scalar(axn[par][:], ta, mv[:, 0:1], rstd, ALU.subtract, ALU.mult),
                   reads=[tb, Bst], writes=[B_axn[par]])
            for ci in range(NCH):
                P.emit("pe", lambda E, ci=ci: E.transpose(pt1[:, ci * 128:(ci + 1) * 128],
                                                          axn[par][:, ci * 128:(ci + 1) * 128], identb[:]),
                       reads=[B_axn[par], B_identb], writes=[B_pt1])
            P.emit("act", lambda E, qt=qt: E.activation(xnT3[:, :, qt * 128:(qt + 1) * 128],
                                                        pt1[:].rearrange("p (c t) -> p c t", c=NCH), AF.Copy),
                   reads=[B_pt1], writes=[B_xnT])

        def qkv_qk(ei):
            pS, BpS = (ps1, B_ps1) if ei % 2 == 0 else (ps2, B_ps2)
            dst_ap = qf3[:, ei, :] if ei < NCH else kf3[:, ei - NCH, :]
            dstb = B_qf if ei < NCH else B_kf
            for ci in range(NCH):
                P.emit("pe", lambda E, ci=ci: E.matmul(pS[:], wiT3[:, ci, ei * 128:(ei + 1) * 128],
                                                       xnT3[:, ci, :], start=(ci == 0), stop=(ci == NCH - 1)),
                       reads=[B_wiT, B_xnT], writes=[BpS])
            P.emit("act", lambda E: E.activation(dst_ap, pS[:], AF.Identity, bias=bias_col[:, ei:ei + 1]),
                   reads=[BpS, B_bcol], writes=[dstb])

        def qkv_v(kt, hf):
            pS, BpS = (ps1, B_ps1) if hf == 0 else (ps2, B_ps2)
            esl = slice(2 * D + hf * 512, 2 * D + (hf + 1) * 512)
            osl = slice(hf * 512, (hf + 1) * 512)
            P.emit("pe", lambda E: E.matmul(pS[:], ones[0:1, 0:128], biasq[0:1, esl],
                                            start=True, stop=False),
                   reads=[B_ones, B_biasq], writes=[BpS])
            for ci in range(NCH):
                P.emit("pe", lambda E, ci=ci: E.matmul(pS[:], xnT3[:, ci, kt * 128:(kt + 1) * 128],
                                                       wiT3[:, ci, esl], start=False, stop=(ci == NCH - 1)),
                       reads=[B_xnT, B_wiT], writes=[BpS])
            P.emit("act", lambda E: E.activation(vt3[:, kt, hf * 8:(hf + 1) * 8, 0:64],
                                                 pS[:].rearrange("p (h e) -> p h e", h=8), AF.Copy),
                   reads=[BpS], writes=[B_vt])

        def pair_scores(seq, h0, qt):
            hc = h0 // 2
            W = (qt + 1) * 128

            def build2(E):
                E.matmul(ps1[:, 0:W], qf3[:, hc, qt * 128:(qt + 1) * 128][0:64, :],
                         kf3[:, hc, 0:W][0:64, :], start=True, stop=True,
                         skip_group_check=True)
                return E.matmul(pB[:, 0:W], qf3[:, hc, qt * 128:(qt + 1) * 128][64:128, :],
                                kf3[:, hc, 0:W][64:128, :], start=True, stop=True,
                                skip_group_check=True)
            P.emit("pe", build2, reads=[B_qf, B_kf], writes=[B_ps1, B_scB],
                   cost=150 + W * 0.42)

        def attn_head_qt(seq, h, qt):
            hc = h // 2
            hp = (h % 2) * 64
            W = (qt + 1) * 128
            par = h % 2
            pbuf, Bp = pb[par], B_p[par]
            pT, BpT = pTs[par], B_pT[par]
            if h % 2 == 0:
                psc, Bsc = ps1, B_ps1
            else:
                psc, Bsc = pB[:, 0:512], B_scB
            ppv, Bpv, pvo = ps2, B_ps2, 0
            aslot = h % 4
            Bast = B_astat[aslot]
            a0 = aslot * 16
            negmx = astat[:, a0 + 1:a0 + 2]
            sm = astat[:, a0 + 2:a0 + 3]; invs = astat[:, a0 + 3:a0 + 4]
            P.emit("pe", lambda E: E.matmul(psc[:, 0:W], qf3[:, hc, qt * 128:(qt + 1) * 128][hp:hp + 64, :],
                                            kf3[:, hc, 0:W][hp:hp + 64, :], start=True, stop=True,
                                            skip_group_check=True),
                   reads=[B_qf, B_kf], writes=[Bsc])
            P.emit("dve", lambda E: E.tensor_tensor(psc[:, W - 128:W], psc[:, W - 128:W], cmask[:], ALU.add),
                   reads=[B_cmask], writes=[Bsc])
            P.emit("dve", lambda E: E.tensor_reduce(negmx, psc[:, 0:W], AX.X, ALU.max, negate=True),
                   reads=[Bsc], writes=[Bast])
            P.emit("act", lambda E: E.activation(pbuf[:, 0:W], psc[:, 0:W], AF.Exp, bias=negmx),
                   reads=[Bsc, Bast], writes=[Bp])

            for kt in range(qt + 1):
                P.emit("pe", lambda E, kt=kt: E.transpose(pt1[:, kt * 128:(kt + 1) * 128],
                                                          pbuf[:, kt * 128:(kt + 1) * 128], identb[:]),
                       reads=[Bp, B_identb], writes=[B_pt1])
            P.emit("act", lambda E: E.activation(pT[:, 0:W], pt1[:, 0:W], AF.Copy),
                   reads=[B_pt1], writes=[BpT])
            for kt in range(qt + 1):
                P.emit("pe", lambda E, kt=kt: E.matmul(ppv[:, pvo:pvo + 65], pT[:, kt * 128:(kt + 1) * 128],
                                                       vt3[:, kt, h, :],
                                                       start=(kt == 0), stop=(kt == qt),
                                                       skip_group_check=True),
                       reads=[BpT, B_vt], writes=[Bpv])
            P.emit("dve", lambda E: E.reciprocal(invs, ppv[:, pvo + 64:pvo + 65]),
                   reads=[Bpv], writes=[Bast], cost=160)
            P.emit("dve", lambda E: E.tensor_scalar(osb3[:, qt % 2, h * 64:(h + 1) * 64], ppv[:, pvo:pvo + 64],
                                                    invs, None, ALU.mult),
                   reads=[Bpv, Bast], writes=[B_osb[qt % 2]])

        def outproj(seq, qt):
            u = uc[0]; uc[0] += 1
            k = seq * SEL + qt
            tb = B_xsel[k]
            ta = xsel3[:, k, :]
            for ci in range(NCH):
                P.emit("pe", lambda E, ci=ci: E.transpose(pt1[:, ci * 128:(ci + 1) * 128],
                                                          osb3[:, qt % 2, ci * 128:(ci + 1) * 128], identb[:]),
                       reads=[B_osb[qt % 2], B_identb], writes=[B_pt1])
            P.emit("act", lambda E: E.activation(oTb[:], pt1[:], AF.Copy),
                   reads=[B_pt1], writes=[B_oT])
            oT3 = oTb[:].rearrange("p (c t) -> p c t", c=NCH)
            pX, BpX = pA, B_pA
            sls = [slice(0, 512), slice(512, 1024)]
            for hf in range(2):
                P.emit("pe", lambda E, sl=sls[hf], pX=pX: E.matmul(pX[:, sl], ones[0:1, 0:128], biaso[0:1, sl],
                                                                   start=True, stop=False, skip_group_check=True),
                       reads=[B_ones, B_biaso], writes=[BpX[hf]], cost=280)
            for ci in range(NCH):
                for hf in range(2):
                    P.emit("pe", lambda E, ci=ci, sl=sls[hf], pX=pX: E.matmul(pX[:, sl], oT3[:, ci, :],
                                                                              woT3[:, ci, sl],
                                                                              start=False, stop=(ci == NCH - 1),
                                                                              skip_group_check=True),
                           reads=[B_oT, B_woT], writes=[BpX[hf]], cost=330)
            for hf in range(2):
                P.emit("dve", lambda E, sl=sls[hf], pX=pX: E.tensor_tensor(ta[:, sl], ta[:, sl], pX[:, sl], ALU.add),
                       reads=[BpX[hf], tb], writes=[tb], cost=660)

        def dump(name, buf, ap):
            if debug and name in dbg:
                P.emit("gp", lambda E: E.dma_start(out=dbg[name][:], in_=ap), reads=[buf], dma="qg")

        def load_tile(g, buf, ap):
            P.emit("sync", lambda E: E.dma_start(out=ap, in_=xin[g * 128:(g + 1) * 128, :]),
                   writes=[buf], dma="qs")

        def store_tile(g, buf, ap):
            P.emit("gp", lambda E: E.dma_start(out=xout[g * 128:(g + 1) * 128, :], in_=ap),
                   reads=[buf], dma="qg")

        for seq in range(NSEQ):
            for qt in range(SEL):
                load_tile(seq * TPS + qt, B_xsel[seq * SEL + qt], xsel3[:, seq * SEL + qt, :])

            A = []
            Fx = []

            for qt in range(SEL):
                A.append(lambda qt=qt: ffn(B_xsel[seq * SEL + qt], xsel3[:, seq * SEL + qt, :]))
            if debug and seq == 0:
                A.append(lambda: dump("d_x1", B_xsel[0], xsel3[:, 0, :]))
            for qt in range(SEL):
                A.append(lambda qt=qt: attn_ln(seq, qt))
            if debug and seq == 0:
                A.append(lambda: dump("d_xnt", B_xnT, xnT[:]))
            for ei in range(2 * NCH):
                A.append(lambda ei=ei: qkv_qk(ei))
            A.append(lambda: P.emit("dve", lambda E: E.memset(vt[:], 1.0), writes=[B_vt], cost=1150))
            for kt in range(SEL):
                for hf in range(2):
                    A.append(lambda kt=kt, hf=hf: qkv_v(kt, hf))
            if debug and seq == 0:
                A.append(lambda: dump("d_q", B_qf, qf[:]))
                A.append(lambda: dump("d_v", B_vt, vt[:]))
            def mk_self2(qt):
                def self2():
                    k = seq * SEL + qt
                    ffn(B_xsel[k], xsel3[:, k, :])
                    store_tile(seq * TPS + qt, B_xsel[k], xsel3[:, k, :])
                return self2
            def head_u(seq_, h, qt):
                if h % 2 == 0:
                    pair_scores(seq_, h, qt)
                attn_head_qt(seq_, h, qt)
            for h in range(H):
                A.append(lambda h=h: head_u(seq, h, 0))
            for qt in range(1, SEL + 1):
                if qt < SEL:
                    for h in range(4):
                        A.append(lambda h=h, qt=qt: head_u(seq, h, qt))
                A.append(lambda qt=qt: outproj(seq, qt - 1))
                A.append(mk_self2(qt - 1))
                if qt < SEL:
                    for h in range(4, H):
                        A.append(lambda h=h, qt=qt: head_u(seq, h, qt))

            nstiles = list(range(SEL, TPS))
            for gi in range(0, len(nstiles), 4):
                grp = nstiles[gi:gi + 4]
                for t in grp:
                    def ldf1(t=t):
                        g = seq * TPS + t
                        slot = t % 4
                        load_tile(g, B_xns[slot], xns3[:, slot, :])
                        ffn(B_xns[slot], xns3[:, slot, :])
                    Fx.append(ldf1)
                for t in grp:
                    def f2st(t=t):
                        g = seq * TPS + t
                        slot = t % 4
                        ffn(B_xns[slot], xns3[:, slot, :])
                        store_tile(g, B_xns[slot], xns3[:, slot, :])
                    Fx.append(f2st)

            ai, fi = 0, 0
            while ai < len(A) or fi < len(Fx):
                for _ in range(4):
                    if ai < len(A):
                        A[ai](); ai += 1
                if fi < len(Fx):
                    Fx[fi](); fi += 1

        plan, cnt = P.build_plan()
        def final_wait(E):
            for j in range(NROT):
                nm = "qg%d" % j
                if cnt[nm]:
                    E.wait_ge(sems[nm], cnt[nm])
        plan["gp"].append(final_wait)

        with nc.Block() as block:
            @block.sync
            def _(E):
                for a in plan["sync"]:
                    a(E)

            @block.tensor
            def _(E):
                for a in plan["pe"]:
                    a(E)

            @block.vector
            def _(E):
                for a in plan["dve"]:
                    a(E)

            @block.scalar
            def _(E):
                for a in plan["act"]:
                    a(E)

            @block.gpsimd
            def _(E):
                for a in plan["gp"]:
                    a(E)

    return nc


_CACHE = {}


def kernel(L0_attn_x, L0_attn_w_in, L0_attn_b_in, L0_attn_w_out, L0_attn_b_out,
           L0_ffn_weight, L0_ffn_bias):
    x = np.ascontiguousarray(np.asarray(L0_attn_x), dtype=np.float32)
    w_in = np.ascontiguousarray(np.asarray(L0_attn_w_in), dtype=np.float32)
    b_in = np.ascontiguousarray(np.asarray(L0_attn_b_in), dtype=np.float32).reshape(1, 3 * D)
    w_out = np.ascontiguousarray(np.asarray(L0_attn_w_out), dtype=np.float32)
    b_out = np.ascontiguousarray(np.asarray(L0_attn_b_out), dtype=np.float32).reshape(1, D)
    fw = np.ascontiguousarray(np.asarray(L0_ffn_weight), dtype=np.float32)
    fb = np.ascontiguousarray(np.asarray(L0_ffn_bias), dtype=np.float32).reshape(1, D)

    if "nc" not in _CACHE:
        _CACHE["nc"] = build_program()
    nc = _CACHE["nc"]

    ident = np.eye(128, dtype=np.float32)
    qi = np.arange(128)
    cmask = np.where(qi[None, :] <= qi[:, None], 0.0, -1e9).astype(np.float32)

    in_maps = []
    for i in range(8):
        xs = x[2 * i:2 * i + 2]
        xp = np.ascontiguousarray(
            xs.reshape(2, 512, 4, D).transpose(0, 2, 1, 3).reshape(NT * 128, D))
        in_maps.append({
            "xin": xp, "w_in": w_in, "b_in": b_in, "w_out": w_out, "b_out": b_out,
            "fw": fw, "fb": fb, "identm": ident, "cmaskm": cmask,
        })

    res = run_bass_kernel_spmd(nc, in_maps, core_ids=list(range(8)))
    out = np.empty((16, 2048, D), dtype=np.float32)
    for i in range(8):
        xo = res.results[i]["xout"]
        out[2 * i:2 * i + 2] = xo.reshape(2, 4, 512, D).transpose(0, 2, 1, 3).reshape(2, 2048, D)
    return out


# version 21
# speedup vs baseline: 1.0462x; 1.0173x over previous
import numpy as np
import concourse.bass as bass
import concourse.mybir as mybir
import concourse.bass_utils as _BU
from concourse.bass_utils import run_bass_kernel_spmd
from contextlib import ExitStack
from pathlib import Path


def _bir_verify_and_optimise_ldw(tmpdir, inp="bir.json", outp="file.neff", arch=None, *,
                                 dve_root=None):
    cmd = [
        _BU.get_walrus_driver(),
        "--pass",
        ",".join(["birverifier", "runtime_memory_reservation", "lower_act", "lower_dve",
                  "lower_ap_offset", "codegen", "neff_packager"]),
        "-i", inp,
        "--neff-output-filename", outp,
        "--enable-birsim=true",
        "--mem-mode=physical",
        "--policy=0",
        "--enable-ldw-opt=true",
        "--assign-static-dmas-to-sp=false",
        "--dram-page-size=%s" % _BU.aot_getenv("NEURON_SCRATCHPAD_PAGE_SIZE", "256"),
        "--enable-neff-debug-info=true",
        "--jobs", "8",
        *_BU.get_walrus_args(_BU.get_bir_arch(tmpdir, inp) if arch is None else arch,
                             tmpdir, dve_root=dve_root),
    ]
    result = _BU.run_command(cmd, cwd=tmpdir)
    if result is not None:
        (Path(tmpdir) / "log.txt").write_text(result.stdout)
    return f"{tmpdir}/{outp}"


_BU.bir_verify_and_optimise = _bir_verify_and_optimise_ldw

F32 = mybir.dt.float32
BF16 = mybir.dt.bfloat16
FP8 = mybir.dt.float8e4
AF = mybir.ActivationFunctionType
ALU = mybir.AluOpType
AX = mybir.AxisListType

D = 1024
NCH = 8
TPS = 16
NSEQ = 2
NT = TPS * NSEQ
SEL = 4
H = 16
DH = 64
M = 512
EPS = 1e-5
NSC = 3

ENG_SEM = {"pe": "pe", "act": "act", "dve": "dve", "gp": "pool", "sync": None}


class Buf:
    __slots__ = ("w", "r")

    def __init__(self):
        self.w = None
        self.r = []

    def __repr__(self):
        return f"Buf(w={self.w})"


class Op:
    __slots__ = ("oid", "eng", "build", "dma", "cost", "deps", "succ", "prio",
                 "qpos", "start", "finish")

    def __init__(self, oid, eng, build, dma, cost, deps):
        self.oid = oid
        self.eng = eng
        self.build = build
        self.dma = dma
        self.cost = cost
        self.deps = deps
        self.succ = []
        self.prio = 0.0
        self.qpos = -1
        self.start = 0.0
        self.finish = 0.0


DMA_INFLIGHT = 8
NROT = 12


class Prog:

    def __init__(self, sems):
        self.sems = sems
        self.ops = []

    def emit(self, eng, build, reads=(), writes=(), dma=None, cost=None):
        if cost is None:
            cost = {"pe": 280.0, "act": 650.0, "dve": 700.0, "gp": 1900.0,
                    "sync": 1500.0}[eng]
            if dma is not None:
                cost = 1500.0
        oid = len(self.ops)
        deps = set()
        for b in reads:
            if b.w is not None:
                deps.add(b.w)
        for b in writes:
            if b.w is not None:
                deps.add(b.w)
            for t in b.r:
                deps.add(t)
        deps.discard(oid)
        op = Op(oid, eng, build, dma, float(cost), deps)
        self.ops.append(op)
        for b in reads:
            b.r.append(oid)
        for b in writes:
            b.w = oid
            b.r = []
        return oid

    def schedule(self):
        import heapq
        ops = self.ops
        n = len(ops)
        for op in ops:
            for d in op.deps:
                ops[d].succ.append(op.oid)
        for op in reversed(ops):
            m = 0.0
            for s in op.succ:
                if ops[s].prio > m:
                    m = ops[s].prio
            op.prio = m + op.cost
        ndep = [len(op.deps) for op in ops]
        eng_time = {e: 0.0 for e in ENG_SEM}
        ready = []
        for op in ops:
            if ndep[op.oid] == 0:
                heapq.heappush(ready, (0.0, -op.prio, op.oid))
        streams = {e: [] for e in ENG_SEM}
        order = []
        while ready:
            rt, nprio, oid = heapq.heappop(ready)
            op = ops[oid]
            start = max(rt, eng_time[op.eng])
            op.start = start
            op.finish = start + op.cost
            eng_time[op.eng] = op.finish
            streams[op.eng].append(oid)
            order.append(oid)
            for s in op.succ:
                ndep[s] -= 1
                if ndep[s] == 0:
                    srt = 0.0
                    se = ops[s].eng
                    for d in ops[s].deps:
                        f = ops[d].finish
                        if ops[d].eng != se or ops[d].dma is not None:
                            f += 400.0
                        if f > srt:
                            srt = f
                    heapq.heappush(ready, (srt, -ops[s].prio, s))
        assert len(order) == n, f"cycle in dep graph: {len(order)}/{n}"
        self.makespan = max(eng_time.values())
        return streams, order

    def build_plan(self):
        streams, order = self.schedule()
        ops = self.ops
        opos = {oid: i for i, oid in enumerate(order)}
        qhist = {"qs": [], "qg": [], "qa": []}
        semof = {}
        for oid in order:
            op = ops[oid]
            if op.dma is None:
                semof[oid] = ENG_SEM[op.eng]
            else:
                q = op.dma
                j = len(qhist[q])
                qhist[q].append(oid)
                op.qpos = j
                semof[oid] = "%s%d" % (q, j % NROT)
        needed = set()
        for oid in order:
            op = ops[oid]
            own = "pe" if (op.eng == "pe" and op.dma is None) else None
            best = {}
            for d in op.deps:
                s = semof[d]
                if s == own:
                    continue
                if s not in best or opos[d] > opos[best[s]]:
                    best[s] = d
            for d in best.values():
                needed.add(d)
            if op.dma is not None and op.qpos >= DMA_INFLIGHT:
                needed.add(qhist[op.dma][op.qpos - DMA_INFLIGHT])
        cnt = {k: 0 for k in self.sems}
        tag = {}
        plan = {e: [] for e in streams}
        marks = {e: {k: 0 for k in self.sems} for e in streams}
        for oid in order:
            op = ops[oid]
            e = op.eng
            s_name = semof[oid]
            do_inc = (op.dma is not None) or (oid in needed)
            inc = 16 if op.dma is not None else 1
            if do_inc:
                cnt[s_name] += inc
            tag[oid] = (s_name, cnt[s_name])
            own = "pe" if (e == "pe" and op.dma is None) else None
            need = {}
            for d in op.deps:
                s, v = tag[d]
                if s == own:
                    continue
                if need.get(s, 0) < v:
                    need[s] = v
            if op.dma is not None and op.qpos >= DMA_INFLIGHT:
                s, v = tag[qhist[op.dma][op.qpos - DMA_INFLIGHT]]
                if need.get(s, 0) < v:
                    need[s] = v
            waits = []
            for s, v in sorted(need.items()):
                if v > 0 and marks[e][s] < v:
                    waits.append((self.sems[s], v))
                    marks[e][s] = v
            sem_h = self.sems[s_name] if do_inc else None

            def action(E, waits=waits, build=op.build, sem_h=sem_h, inc=inc):
                for sh, v in waits:
                    E.wait_ge(sh, v)
                ins = build(E)
                if sem_h is not None:
                    ins.then_inc(sem_h, inc)

            plan[e].append(action)
        return plan, cnt


def build_program(debug=False):
    nc = bass.Bass("TRN2", target_bir_lowering=False)

    xin = nc.dram_tensor("xin", [NT * 128, D], F32, kind="ExternalInput").ap()
    w_in_h = nc.dram_tensor("w_in", [3 * D, D], F32, kind="ExternalInput").ap()
    b_in_h = nc.dram_tensor("b_in", [1, 3 * D], F32, kind="ExternalInput").ap()
    w_out_h = nc.dram_tensor("w_out", [D, D], F32, kind="ExternalInput").ap()
    b_out_h = nc.dram_tensor("b_out", [1, D], F32, kind="ExternalInput").ap()
    fw_h = nc.dram_tensor("fw", [D, D], F32, kind="ExternalInput").ap()
    fb_h = nc.dram_tensor("fb", [1, D], F32, kind="ExternalInput").ap()
    id_h = nc.dram_tensor("identm", [128, 128], F32, kind="ExternalInput").ap()
    cm_h = nc.dram_tensor("cmaskm", [128, 128], F32, kind="ExternalInput").ap()
    xout = nc.dram_tensor("xout", [NT * 128, D], F32, kind="ExternalOutput").ap()
    dbg = {}
    if debug:
        for nm, shape, dt in [("d_x1", [128, D], F32), ("d_xnt", [128, 8 * M], BF16),
                              ("d_q", [128, 8 * M], BF16), ("d_v", [128, SEL * D], BF16),
                              ("d_o", [128, D], BF16), ("d_x2", [128, D], F32)]:
            dbg[nm] = nc.dram_tensor(nm, shape, dt, kind="ExternalOutput").ap()

    with ExitStack() as ES:
        sb = lambda name, shape, dt: ES.enter_context(nc.sbuf_tensor(name, shape, dt))
        ps = lambda name, shape, dt: ES.enter_context(nc.psum_tensor(name, shape, dt))

        wiT = sb("wiT", [128, NCH * 3 * D], BF16)
        woT = sb("woT", [128, NCH * D], BF16)
        fwT = sb("fwT", [128, NCH * D], BF16)
        biasq = sb("biasq", [1, 3 * D], BF16)
        biaso = sb("biaso", [1, D], BF16)
        biasf = sb("biasf", [1, D], BF16)
        ones = sb("onesr", [1, 512], BF16)
        identf = sb("identf", [128, 128], F32)
        identb = sb("identb", [128, 128], BF16)
        cmask = sb("cmaskt", [128, 128], F32)
        epst = sb("epst", [128, 1], F32)
        biasfF = sb("biasfF", [128, D], BF16)
        bias_col = sb("bias_col", [128, 16], F32)

        xsel = sb("xsel", [128, 8 * D], F32)
        xns = sb("xns", [128, 4 * D], F32)

        xnb = sb("xnb", [128, D], F32)
        z8b = sb("z8b", [128, D], FP8)
        xdb = [sb("xdb%d" % i, [128, D], BF16) for i in range(NSC)]
        xdT = [sb("xdT%d" % i, [128, D], BF16) for i in range(NSC)]
        statb = [sb("statb%d" % i, [128, 128], F32) for i in range(NSC)]
        axn = [sb("axn0", [128, D], BF16)]
        oTb = sb("oTb", [128, D], BF16)

        xnT = sb("xnT", [128, NCH * M], BF16)
        qf = sb("qf", [128, NCH * M], BF16)
        kf = sb("kf", [128, NCH * M], BF16)
        vt = sb("vt", [128, SEL * H * 65], BF16)
        osb = sb("osb", [128, 2 * D], BF16)
        pb = [sb("pb%d" % i, [128, M], BF16) for i in range(2)]
        pTs = [sb("pTs%d" % i, [128, M], BF16) for i in range(2)]
        astat = sb("astat", [128, 64], F32)

        pA = ps("pA", [128, D], F32)
        pB = ps("pB", [128, D], F32)
        pt1 = ps("pt1", [128, D], BF16)
        pt2 = ps("pt2", [128, D], BF16)
        ps1 = ps("ps1", [128, 512], F32)
        ps2 = ps("ps2", [128, 512], F32)

        sems = {}
        for name in ("pe", "act", "dve", "pool"):
            sems[name] = ES.enter_context(nc.semaphore(name))
        for q in ("qs", "qg", "qa"):
            for j in range(NROT):
                sems["%s%d" % (q, j)] = ES.enter_context(nc.semaphore("%s%d" % (q, j)))

        P = Prog(sems)

        B_wiT = Buf(); B_woT = Buf(); B_fwT = Buf()
        B_biasq = Buf(); B_biaso = Buf(); B_biasf = Buf(); B_ones = Buf()
        B_identf = Buf(); B_identb = Buf(); B_cmask = Buf()
        B_biasfF = Buf(); B_bcol = Buf()
        B_eps = Buf()
        B_xsel = [Buf() for _ in range(8)]
        B_xns = [Buf() for _ in range(4)]
        B_xnb = Buf()
        B_z8 = Buf()
        B_xd = [Buf() for _ in range(NSC)]
        B_xdT = [Buf() for _ in range(NSC)]
        B_stat = [Buf() for _ in range(NSC)]
        B_axn = [Buf()]
        B_oT = Buf()
        B_xnT = Buf(); B_qf = Buf(); B_kf = Buf(); B_vt = Buf(); B_osb = [Buf(), Buf()]
        B_p = [Buf() for _ in range(2)]
        B_pT = [Buf() for _ in range(2)]
        B_astat = [Buf() for _ in range(4)]
        B_pA = [Buf(), Buf()]; B_scB = Buf(); B_pt1 = Buf(); B_pt2 = Buf()
        B_ps1 = Buf(); B_ps2 = Buf()

        wiT3 = wiT[:].rearrange("p (c e) -> p c e", c=NCH)
        woT3 = woT[:].rearrange("p (c e) -> p c e", c=NCH)
        fwT3 = fwT[:].rearrange("p (c e) -> p c e", c=NCH)
        xnT3 = xnT[:].rearrange("p (c t) -> p c t", c=NCH)
        qf3 = qf[:].rearrange("p (c t) -> p c t", c=NCH)
        kf3 = kf[:].rearrange("p (c t) -> p c t", c=NCH)
        vt3 = vt[:].rearrange("p (k h e) -> p k h e", k=SEL, h=H)
        osb3 = osb[:].rearrange("p (q c) -> p q c", q=2)
        xsel3 = xsel[:].rearrange("p (n c) -> p n c", n=8)
        xns3 = xns[:].rearrange("p (n c) -> p n c", n=4)

        wstage = [xns3[:, 0, :], xns3[:, 1, :], xns3[:, 2, :]]
        B_wstage = [B_xns[0], B_xns[1], B_xns[2]]

        uc = [0]

        P.emit("sync", lambda E: E.dma_start(out=identf[:], in_=id_h[:]), writes=[B_identf], dma="qs")
        P.emit("sync", lambda E: E.dma_start(out=cmask[:], in_=cm_h[:]), writes=[B_cmask], dma="qs")
        P.emit("act", lambda E: E.activation(identb[:], identf[:], AF.Copy),
               reads=[B_identf], writes=[B_identb])
        P.emit("dve", lambda E: E.memset(ones[:], 1.0), writes=[B_ones])
        P.emit("dve", lambda E: E.memset(epst[:], EPS), writes=[B_eps])

        def load_bias(src_ap, n, dstb, scales, Bdst):
            for c0 in range(0, n, D):
                sc = scales(c0)
                P.emit("sync", lambda E, c0=c0, src_ap=src_ap: E.dma_start(out=wstage[0][0:1, :], in_=src_ap[0:1, c0:c0 + D]),
                       writes=[B_wstage[0]], dma="qs")
                P.emit("act", lambda E, c0=c0, sc=sc, dstb=dstb: E.activation(dstb[0:1, c0:c0 + D], wstage[0][0:1, :],
                                                                   AF.Copy, scale=sc),
                       reads=[B_wstage[0]], writes=[Bdst])
        load_bias(b_in_h, 3 * D, biasq, lambda c0: 0.125 if c0 < D else 1.0, B_biasq)
        load_bias(b_out_h, D, biaso, lambda c0: 1.0, B_biaso)
        load_bias(fb_h, D, biasf, lambda c0: 0.5, B_biasf)

        for hf in range(2):
            P.emit("pe", lambda E, hf=hf: E.matmul(ps1[:], ones[0:1, 0:128],
                                                   biasf[0:1, hf * 512:(hf + 1) * 512],
                                                   start=True, stop=True),
                   reads=[B_ones, B_biasf], writes=[B_ps1], cost=280)
            P.emit("act", lambda E, hf=hf: E.activation(biasfF[:, hf * 512:(hf + 1) * 512], ps1[:], AF.Copy),
                   reads=[B_ps1], writes=[B_biasfF], cost=600)
        for j in range(16):
            P.emit("pe", lambda E, j=j: E.matmul(ps2[:, j:j + 1], biasq[0:1, j * 128:(j + 1) * 128],
                                                 ones[0:1, 0:1], start=(j == 0), stop=(j == 15),
                                                 skip_group_check=True),
                   reads=[B_biasq, B_ones], writes=[B_ps2], cost=100)
        P.emit("act", lambda E: E.activation(bias_col[:], ps2[:, 0:16], AF.Copy),
               reads=[B_ps2], writes=[B_bcol], cost=300)

        def quantize_mx(src_buf, src_ap, out_buf, out_ap, st, Bstat, dq_scale):
            am = st[:, 32:64]
            m3 = st[:, 64:96]
            inv = st[:, 96:128]
            src3 = src_ap.rearrange("p (b c) -> p b c", c=32)
            P.emit("dve", lambda E: E.tensor_reduce(am, src3, AX.X, ALU.max, apply_absolute_value=True),
                   reads=[src_buf], writes=[Bstat])
            P.emit("dve", lambda E: E.tensor_scalar(inv, am, float(64.0 / 7.5), float(64e-30), ALU.mult, ALU.max),
                   reads=[Bstat], writes=[Bstat])
            P.emit("dve", lambda E: E.reciprocal(inv, inv), reads=[Bstat], writes=[Bstat])
            P.emit("dve", lambda E: E.tensor_scalar(m3, am, float(64.0 * dq_scale / 7.5), float(64.0 * dq_scale * 1e-30), ALU.mult, ALU.max),
                   reads=[Bstat], writes=[Bstat])
            inv_b = inv.unsqueeze(2).broadcast_to([128, 32, 32])
            m3_b = m3.unsqueeze(2).broadcast_to([128, 32, 32])
            z3 = z8b[:].rearrange("p (b c) -> p b c", c=32)
            o3 = out_ap.rearrange("p (b c) -> p b c", c=32)
            P.emit("gp", lambda E: E.tensor_tensor(z3, src3, inv_b, ALU.mult),
                   reads=[src_buf, Bstat], writes=[B_z8], cost=1950)
            P.emit("gp", lambda E: E.tensor_tensor(o3, z3, m3_b, ALU.mult),
                   reads=[B_z8, Bstat], writes=[out_buf], cost=1950)

        def transpose_tile(src_buf, src_ap, dst_buf, dst_ap_3d, pt, Bpt):
            for ci in range(NCH):
                P.emit("pe", lambda E, ci=ci: E.transpose(pt[:, ci * 128:(ci + 1) * 128],
                                                          src_ap[:, ci * 128:(ci + 1) * 128], identb[:]),
                       reads=[src_buf, B_identb], writes=[Bpt], cost=240)
            P.emit("act", lambda E: E.activation(dst_ap_3d, pt[:].rearrange("p (c t) -> p c t", c=NCH), AF.Copy),
                   reads=[Bpt], writes=[dst_buf], cost=740)

        def weight_unit(i, src_h, row, dstb, dst3, mode, sc=1.0):
            wst, Bwst = wstage[i % 3], B_wstage[i % 3]
            par = i % NSC
            P.emit("act", lambda E: E.dma_start(out=wst, in_=src_h[row * 128:(row + 1) * 128, :]),
                   writes=[Bwst], dma="qa")
            if mode == "quant":
                quantize_mx(Bwst, wst, B_xd[par], xdb[par][:], statb[par], B_stat[par], 0.5)
            else:
                P.emit("act", lambda E, sc=sc, par=par: E.activation(xdb[par][:], wst, AF.Copy, scale=sc),
                       reads=[Bwst], writes=[B_xd[par]])
            pt, Bpt = (pt1, B_pt1) if i % 2 == 0 else (pt2, B_pt2)
            transpose_tile(B_xd[par], xdb[par][:], dstb, dst3, pt, Bpt)

        wu = 0
        for oi in range(NCH):
            weight_unit(wu, fw_h, oi, B_fwT, fwT3[:, :, oi * 128:(oi + 1) * 128], "quant"); wu += 1
        for oi in range(NCH):
            weight_unit(wu, w_out_h, oi, B_woT, woT3[:, :, oi * 128:(oi + 1) * 128], "cast"); wu += 1
        for ei in range(3 * NCH):
            weight_unit(wu, w_in_h, ei, B_wiT, wiT3[:, :, ei * 128:(ei + 1) * 128], "cast",
                        0.125 if ei < NCH else 1.0); wu += 1

        def ffn(tile_buf, tile_ap):
            u = uc[0]; uc[0] += 1
            par = u % NSC
            st = statb[par]; Bst = B_stat[par]
            bns = st[:, 0:12]; mv = st[:, 12:14]; lnv = st[:, 16:17]; rstd = st[:, 17:18]
            P.emit("dve", lambda E: E.bn_stats(bns[:, 0:6], tile_ap[:, 0:512]),
                   reads=[tile_buf], writes=[Bst])
            P.emit("dve", lambda E: E.bn_stats(bns[:, 6:12], tile_ap[:, 512:1024]),
                   reads=[tile_buf], writes=[Bst])
            P.emit("dve", lambda E: E.bn_aggr(mv, bns), reads=[Bst], writes=[Bst])
            P.emit("act", lambda E: E.activation(lnv, mv[:, 1:2], AF.Ln, bias=epst[:, 0:1]),
                   reads=[Bst, B_eps], writes=[Bst])
            P.emit("act", lambda E: E.activation(rstd, lnv, AF.Exp, scale=-0.5),
                   reads=[Bst], writes=[Bst])
            P.emit("dve", lambda E: E.tensor_scalar(xnb[:], tile_ap, mv[:, 0:1], rstd, ALU.subtract, ALU.mult),
                   reads=[tile_buf, Bst], writes=[B_xnb])
            quantize_mx(B_xnb, xnb[:], B_xd[par], xdb[par][:], st, Bst, 1.0)
            transpose_tile(B_xd[par], xdb[par][:], B_xdT[par],
                           xdT[par][:].rearrange("p (c t) -> p c t", c=NCH), pt2, B_pt2)
            xdT3 = xdT[par][:].rearrange("p (c t) -> p c t", c=NCH)
            pX, BpX = pA, B_pA
            sls = [slice(0, 512), slice(512, 1024)]
            for ci in range(NCH):
                for hf in range(2):
                    P.emit("pe", lambda E, ci=ci, sl=sls[hf], pX=pX: E.matmul(
                        pX[:, sl], xdT3[:, ci, :], fwT3[:, ci, sl], start=(ci == 0), stop=(ci == NCH - 1),
                        skip_group_check=True),
                           reads=[B_xdT[par], B_fwT], writes=[BpX[hf]], cost=330)
            P.emit("gp", lambda E: E.tensor_tensor(tile_ap, tile_ap, biasfF[:], ALU.add),
                   reads=[B_biasfF, tile_buf], writes=[tile_buf], cost=1950)
            for hf in range(2):
                P.emit("dve", lambda E, sl=sls[hf], pX=pX: E.tensor_tensor(tile_ap[:, sl], tile_ap[:, sl],
                                                                           pX[:, sl], ALU.add),
                       reads=[BpX[hf], tile_buf], writes=[tile_buf], cost=660)

        def attn_ln(seq, qt):
            par = 0
            k = seq * SEL + qt
            tb = B_xsel[k]
            ta = xsel3[:, k, :]
            u = uc[0]; uc[0] += 1
            spar = u % NSC
            st = statb[spar]; Bst = B_stat[spar]
            bns = st[:, 0:12]; mv = st[:, 12:14]; lnv = st[:, 16:17]; rstd = st[:, 17:18]
            P.emit("dve", lambda E: E.bn_stats(bns[:, 0:6], ta[:, 0:512]), reads=[tb], writes=[Bst])
            P.emit("dve", lambda E: E.bn_stats(bns[:, 6:12], ta[:, 512:1024]), reads=[tb], writes=[Bst])
            P.emit("dve", lambda E: E.bn_aggr(mv, bns), reads=[Bst], writes=[Bst])
            P.emit("act", lambda E: E.activation(lnv, mv[:, 1:2], AF.Ln, bias=epst[:, 0:1]),
                   reads=[Bst, B_eps], writes=[Bst])
            P.emit("act", lambda E: E.activation(rstd, lnv, AF.Exp, scale=-0.5),
                   reads=[Bst], writes=[Bst])
            P.emit("dve", lambda E: E.tensor_scalar(axn[par][:], ta, mv[:, 0:1], rstd, ALU.subtract, ALU.mult),
                   reads=[tb, Bst], writes=[B_axn[par]])
            for ci in range(NCH):
                P.emit("pe", lambda E, ci=ci: E.transpose(pt1[:, ci * 128:(ci + 1) * 128],
                                                          axn[par][:, ci * 128:(ci + 1) * 128], identb[:]),
                       reads=[B_axn[par], B_identb], writes=[B_pt1])
            P.emit("act", lambda E, qt=qt: E.activation(xnT3[:, :, qt * 128:(qt + 1) * 128],
                                                        pt1[:].rearrange("p (c t) -> p c t", c=NCH), AF.Copy),
                   reads=[B_pt1], writes=[B_xnT])

        def qkv_qk(ei):
            pS, BpS = (ps1, B_ps1) if ei % 2 == 0 else (ps2, B_ps2)
            dst_ap = qf3[:, ei, :] if ei < NCH else kf3[:, ei - NCH, :]
            dstb = B_qf if ei < NCH else B_kf
            for ci in range(NCH):
                P.emit("pe", lambda E, ci=ci: E.matmul(pS[:], wiT3[:, ci, ei * 128:(ei + 1) * 128],
                                                       xnT3[:, ci, :], start=(ci == 0), stop=(ci == NCH - 1)),
                       reads=[B_wiT, B_xnT], writes=[BpS])
            P.emit("act", lambda E: E.activation(dst_ap, pS[:], AF.Identity, bias=bias_col[:, ei:ei + 1]),
                   reads=[BpS, B_bcol], writes=[dstb])

        def qkv_v(kt, hf):
            pS, BpS = (ps1, B_ps1) if hf == 0 else (ps2, B_ps2)
            esl = slice(2 * D + hf * 512, 2 * D + (hf + 1) * 512)
            osl = slice(hf * 512, (hf + 1) * 512)
            P.emit("pe", lambda E: E.matmul(pS[:], ones[0:1, 0:128], biasq[0:1, esl],
                                            start=True, stop=False),
                   reads=[B_ones, B_biasq], writes=[BpS])
            for ci in range(NCH):
                P.emit("pe", lambda E, ci=ci: E.matmul(pS[:], xnT3[:, ci, kt * 128:(kt + 1) * 128],
                                                       wiT3[:, ci, esl], start=False, stop=(ci == NCH - 1)),
                       reads=[B_xnT, B_wiT], writes=[BpS])
            P.emit("act", lambda E: E.activation(vt3[:, kt, hf * 8:(hf + 1) * 8, 0:64],
                                                 pS[:].rearrange("p (h e) -> p h e", h=8), AF.Copy),
                   reads=[BpS], writes=[B_vt])

        def pair_scores(seq, h0, qt):
            hc = h0 // 2
            W = (qt + 1) * 128

            def build2(E):
                E.matmul(ps1[:, 0:W], qf3[:, hc, qt * 128:(qt + 1) * 128][0:64, :],
                         kf3[:, hc, 0:W][0:64, :], start=True, stop=True,
                         skip_group_check=True)
                return E.matmul(pB[:, 0:W], qf3[:, hc, qt * 128:(qt + 1) * 128][64:128, :],
                                kf3[:, hc, 0:W][64:128, :], start=True, stop=True,
                                skip_group_check=True)
            P.emit("pe", build2, reads=[B_qf, B_kf], writes=[B_ps1, B_scB],
                   cost=150 + W * 0.42)

        def attn_head_qt(seq, h, qt):
            hc = h // 2
            hp = (h % 2) * 64
            W = (qt + 1) * 128
            par = h % 2
            pbuf, Bp = pb[par], B_p[par]
            pT, BpT = pTs[par], B_pT[par]
            if h % 2 == 0:
                psc, Bsc = ps1, B_ps1
            else:
                psc, Bsc = pB[:, 0:512], B_scB
            ppv, Bpv, pvo = ps2, B_ps2, 0
            aslot = h % 4
            Bast = B_astat[aslot]
            a0 = aslot * 16
            negmx = astat[:, a0 + 1:a0 + 2]
            sm = astat[:, a0 + 2:a0 + 3]; invs = astat[:, a0 + 3:a0 + 4]
            P.emit("pe", lambda E: E.matmul(psc[:, 0:W], qf3[:, hc, qt * 128:(qt + 1) * 128][hp:hp + 64, :],
                                            kf3[:, hc, 0:W][hp:hp + 64, :], start=True, stop=True,
                                            skip_group_check=True),
                   reads=[B_qf, B_kf], writes=[Bsc])
            P.emit("dve", lambda E: E.tensor_tensor(psc[:, W - 128:W], psc[:, W - 128:W], cmask[:], ALU.add),
                   reads=[B_cmask], writes=[Bsc])
            P.emit("dve", lambda E: E.tensor_reduce(negmx, psc[:, 0:W], AX.X, ALU.max, negate=True),
                   reads=[Bsc], writes=[Bast])
            P.emit("act", lambda E: E.activation(pbuf[:, 0:W], psc[:, 0:W], AF.Exp, bias=negmx),
                   reads=[Bsc, Bast], writes=[Bp])

            for kt in range(qt + 1):
                P.emit("pe", lambda E, kt=kt: E.transpose(pt1[:, kt * 128:(kt + 1) * 128],
                                                          pbuf[:, kt * 128:(kt + 1) * 128], identb[:]),
                       reads=[Bp, B_identb], writes=[B_pt1])
            P.emit("act", lambda E: E.activation(pT[:, 0:W], pt1[:, 0:W], AF.Copy),
                   reads=[B_pt1], writes=[BpT])
            for kt in range(qt + 1):
                P.emit("pe", lambda E, kt=kt: E.matmul(ppv[:, pvo:pvo + 65], pT[:, kt * 128:(kt + 1) * 128],
                                                       vt3[:, kt, h, :],
                                                       start=(kt == 0), stop=(kt == qt),
                                                       skip_group_check=True),
                       reads=[BpT, B_vt], writes=[Bpv])
            P.emit("dve", lambda E: E.reciprocal(invs, ppv[:, pvo + 64:pvo + 65]),
                   reads=[Bpv], writes=[Bast], cost=160)
            P.emit("dve", lambda E: E.tensor_scalar(osb3[:, qt % 2, h * 64:(h + 1) * 64], ppv[:, pvo:pvo + 64],
                                                    invs, None, ALU.mult),
                   reads=[Bpv, Bast], writes=[B_osb[qt % 2]])

        def outproj(seq, qt):
            u = uc[0]; uc[0] += 1
            k = seq * SEL + qt
            tb = B_xsel[k]
            ta = xsel3[:, k, :]
            for ci in range(NCH):
                P.emit("pe", lambda E, ci=ci: E.transpose(pt1[:, ci * 128:(ci + 1) * 128],
                                                          osb3[:, qt % 2, ci * 128:(ci + 1) * 128], identb[:]),
                       reads=[B_osb[qt % 2], B_identb], writes=[B_pt1])
            P.emit("act", lambda E: E.activation(oTb[:], pt1[:], AF.Copy),
                   reads=[B_pt1], writes=[B_oT])
            oT3 = oTb[:].rearrange("p (c t) -> p c t", c=NCH)
            pX, BpX = pA, B_pA
            sls = [slice(0, 512), slice(512, 1024)]
            for hf in range(2):
                P.emit("pe", lambda E, sl=sls[hf], pX=pX: E.matmul(pX[:, sl], ones[0:1, 0:128], biaso[0:1, sl],
                                                                   start=True, stop=False, skip_group_check=True),
                       reads=[B_ones, B_biaso], writes=[BpX[hf]], cost=280)
            for ci in range(NCH):
                for hf in range(2):
                    P.emit("pe", lambda E, ci=ci, sl=sls[hf], pX=pX: E.matmul(pX[:, sl], oT3[:, ci, :],
                                                                              woT3[:, ci, sl],
                                                                              start=False, stop=(ci == NCH - 1),
                                                                              skip_group_check=True),
                           reads=[B_oT, B_woT], writes=[BpX[hf]], cost=330)
            for hf in range(2):
                P.emit("dve", lambda E, sl=sls[hf], pX=pX: E.tensor_tensor(ta[:, sl], ta[:, sl], pX[:, sl], ALU.add),
                       reads=[BpX[hf], tb], writes=[tb], cost=660)

        def dump(name, buf, ap):
            if debug and name in dbg:
                P.emit("gp", lambda E: E.dma_start(out=dbg[name][:], in_=ap), reads=[buf], dma="qg")

        def load_tile(g, buf, ap):
            P.emit("sync", lambda E: E.dma_start(out=ap, in_=xin[g * 128:(g + 1) * 128, :]),
                   writes=[buf], dma="qs")

        def store_tile(g, buf, ap):
            P.emit("gp", lambda E: E.dma_start(out=xout[g * 128:(g + 1) * 128, :], in_=ap),
                   reads=[buf], dma="qg")

        for seq in range(NSEQ):
            for qt in range(SEL):
                load_tile(seq * TPS + qt, B_xsel[seq * SEL + qt], xsel3[:, seq * SEL + qt, :])

            A = []
            Fx = []

            for qt in range(SEL):
                A.append(lambda qt=qt: ffn(B_xsel[seq * SEL + qt], xsel3[:, seq * SEL + qt, :]))
            if debug and seq == 0:
                A.append(lambda: dump("d_x1", B_xsel[0], xsel3[:, 0, :]))
            for qt in range(SEL):
                A.append(lambda qt=qt: attn_ln(seq, qt))
            if debug and seq == 0:
                A.append(lambda: dump("d_xnt", B_xnT, xnT[:]))
            for ei in range(2 * NCH):
                A.append(lambda ei=ei: qkv_qk(ei))
            A.append(lambda: P.emit("dve", lambda E: E.memset(vt[:], 1.0), writes=[B_vt], cost=1150))
            for kt in range(SEL):
                for hf in range(2):
                    A.append(lambda kt=kt, hf=hf: qkv_v(kt, hf))
            if debug and seq == 0:
                A.append(lambda: dump("d_q", B_qf, qf[:]))
                A.append(lambda: dump("d_v", B_vt, vt[:]))
            def mk_self2(qt):
                def self2():
                    k = seq * SEL + qt
                    ffn(B_xsel[k], xsel3[:, k, :])
                    store_tile(seq * TPS + qt, B_xsel[k], xsel3[:, k, :])
                return self2
            def head_u(seq_, h, qt):
                if h % 2 == 0:
                    pair_scores(seq_, h, qt)
                attn_head_qt(seq_, h, qt)
            for h in range(H):
                A.append(lambda h=h: head_u(seq, h, 0))
            for qt in range(1, SEL + 1):
                if qt < SEL:
                    for h in range(4):
                        A.append(lambda h=h, qt=qt: head_u(seq, h, qt))
                A.append(lambda qt=qt: outproj(seq, qt - 1))
                A.append(mk_self2(qt - 1))
                if qt < SEL:
                    for h in range(4, H):
                        A.append(lambda h=h, qt=qt: head_u(seq, h, qt))

            nstiles = list(range(SEL, TPS))
            for gi in range(0, len(nstiles), 4):
                grp = nstiles[gi:gi + 4]
                for t in grp:
                    def ldf1(t=t):
                        g = seq * TPS + t
                        slot = t % 4
                        load_tile(g, B_xns[slot], xns3[:, slot, :])
                        ffn(B_xns[slot], xns3[:, slot, :])
                    Fx.append(ldf1)
                for t in grp:
                    def f2st(t=t):
                        g = seq * TPS + t
                        slot = t % 4
                        ffn(B_xns[slot], xns3[:, slot, :])
                        store_tile(g, B_xns[slot], xns3[:, slot, :])
                    Fx.append(f2st)

            ai, fi = 0, 0
            while ai < len(A) or fi < len(Fx):
                for _ in range(4):
                    if ai < len(A):
                        A[ai](); ai += 1
                if fi < len(Fx):
                    Fx[fi](); fi += 1

        plan, cnt = P.build_plan()
        def final_wait(E):
            for j in range(NROT):
                nm = "qg%d" % j
                if cnt[nm]:
                    E.wait_ge(sems[nm], cnt[nm])
        plan["gp"].append(final_wait)

        with nc.Block() as block:
            @block.sync
            def _(E):
                for a in plan["sync"]:
                    a(E)

            @block.tensor
            def _(E):
                for a in plan["pe"]:
                    a(E)

            @block.vector
            def _(E):
                for a in plan["dve"]:
                    a(E)

            @block.scalar
            def _(E):
                for a in plan["act"]:
                    a(E)

            @block.gpsimd
            def _(E):
                for a in plan["gp"]:
                    a(E)

    return nc


_CACHE = {}


def kernel(L0_attn_x, L0_attn_w_in, L0_attn_b_in, L0_attn_w_out, L0_attn_b_out,
           L0_ffn_weight, L0_ffn_bias):
    x = np.ascontiguousarray(np.asarray(L0_attn_x), dtype=np.float32)
    w_in = np.ascontiguousarray(np.asarray(L0_attn_w_in), dtype=np.float32)
    b_in = np.ascontiguousarray(np.asarray(L0_attn_b_in), dtype=np.float32).reshape(1, 3 * D)
    w_out = np.ascontiguousarray(np.asarray(L0_attn_w_out), dtype=np.float32)
    b_out = np.ascontiguousarray(np.asarray(L0_attn_b_out), dtype=np.float32).reshape(1, D)
    fw = np.ascontiguousarray(np.asarray(L0_ffn_weight), dtype=np.float32)
    fb = np.ascontiguousarray(np.asarray(L0_ffn_bias), dtype=np.float32).reshape(1, D)

    if "nc" not in _CACHE:
        _CACHE["nc"] = build_program()
    nc = _CACHE["nc"]

    ident = np.eye(128, dtype=np.float32)
    qi = np.arange(128)
    cmask = np.where(qi[None, :] <= qi[:, None], 0.0, -1e9).astype(np.float32)

    in_maps = []
    for i in range(8):
        xs = x[2 * i:2 * i + 2]
        xp = np.ascontiguousarray(
            xs.reshape(2, 512, 4, D).transpose(0, 2, 1, 3).reshape(NT * 128, D))
        in_maps.append({
            "xin": xp, "w_in": w_in, "b_in": b_in, "w_out": w_out, "b_out": b_out,
            "fw": fw, "fb": fb, "identm": ident, "cmaskm": cmask,
        })

    res = run_bass_kernel_spmd(nc, in_maps, core_ids=list(range(8)))
    out = np.empty((16, 2048, D), dtype=np.float32)
    for i in range(8):
        xo = res.results[i]["xout"]
        out[2 * i:2 * i + 2] = xo.reshape(2, 4, 512, D).transpose(0, 2, 1, 3).reshape(2, 2048, D)
    return out
